# version 10
# baseline (speedup 1.0000x reference)
import contextlib
import numpy as np
import ml_dtypes
import concourse.bass as bass
import concourse.mybir as mybir
from concourse.bass_utils import run_bass_kernel_spmd

F32 = mybir.dt.float32
BF16 = mybir.dt.bfloat16
AF = mybir.ActivationFunctionType
ALU = mybir.AluOpType
AX = mybir.AxisListType

D = 1024
KCH = 8
S_FULL = 8192
TCORE = 4096
TSLICE = 2048
NSLICE = 2
NBLK = 4
BT = 512
NSUB = 4
H = 16
HDIM = 64
LH = 4
LD = 256
NLAT = 64
EXP = 8
EPS = 1e-5

DEBUG = {}


class DSem:

    def __init__(self, handle):
        self.sem = handle
        self.n = 0


class Rec:
    def __init__(self, sem, step=1):
        self.sem = sem
        self.step = step
        self.n = 0
        self.ops = []

    def dma(self, fn, ds):
        ds.n += 16
        self.ops.append((lambda e, f=fn, d=ds: f(e).then_inc(d.sem, 16), -2))
        return ds.n

    def emit(self, fn, inc=False):
        if inc:
            self.n += self.step
            self.ops.append((fn, self.step))
            return self.n
        self.ops.append((fn, 0))
        return self.n

    def milestone(self):
        fn, inc = self.ops[-1]
        assert inc == 0, "last op already incs/waits"
        self.n += self.step
        self.ops[-1] = (fn, self.step)
        return self.n

    def wait(self, rec, val):
        if val <= 0:
            return
        self.ops.append((lambda e, s=rec.sem, v=val: e.wait_ge(s, v), -1))

    def wait_all(self, recs_or_dsems):
        for r in recs_or_dsems:
            self.wait(r, r.n)

    def flush(self, section):
        ops = self.ops
        self.ops = []
        if not ops:
            return

        def body(e):
            for fn, inc in ops:
                ins = fn(e)
                if inc > 0:
                    ins.then_inc(self.sem, inc)
        section(body)


def build_nc(dbgcfg=None, stop_after=None):
    dbgcfg = dbgcfg or {}
    nc = bass.Bass("TRN2", target_bir_lowering=False)

    def din(name, shape, dt=F32):
        return nc.dram_tensor(name, list(shape), dt, kind="ExternalInput").ap()

    x_own = din("x_own", [TCORE, D])
    x_full = din("x_full", [S_FULL, D])
    x_halo = din("x_halo", [128, D])
    qkv_w = din("qkv_w", [D, 3 * D]); qkv_b = din("qkv_b", [3 * D])
    rq_w = din("rq_w", [D, LD]); rq_b = din("rq_b", [LD])
    rkv_w = din("rkv_w", [D, 2 * LD]); rkv_b = din("rkv_b", [2 * LD])
    rout_w = din("rout_w", [LD, D]); rout_b = din("rout_b", [D])
    out_w = din("out_w", [D, D]); out_b = din("out_b", [D])
    gl_w = din("gl_w", [HDIM, 1]); gr_w = din("gr_w", [HDIM, 1])
    gate_w = din("gate_w", [EXP, D]); gate_b = din("gate_b", [EXP])
    w1 = din("w1", [EXP, D, D]); b1 = din("b1", [EXP, D])
    w3 = din("w3", [EXP, D, D]); b3 = din("b3", [EXP, D])
    w2 = din("w2", [EXP, D, D]); b2 = din("b2", [EXP, D])
    ws1 = din("ws1", [2 * D, D]); bs1 = din("bs1", [2 * D])
    ws3 = din("ws3", [2 * D, D]); bs3 = din("bs3", [2 * D])
    ws2 = din("ws2", [D, 2 * D]); bs2 = din("bs2", [D])
    identf = din("identf", [128, 128])
    maskTd = din("maskTd", [2, 128, 256], BF16)
    mld = din("mld", [NLAT, 32])
    ehmatd = din("ehmatd", [16, 8, 128], BF16)
    e8d = din("e8d", [8, 8, 128], BF16)

    y_out = nc.dram_tensor("out", [TCORE, D], F32, kind="ExternalOutput").ap()

    def scr(name, shape, dt=BF16):
        return nc.dram_tensor(name, list(shape), dt, kind="Internal").ap()

    qkv_wb = scr("qkv_wb", [D, 3 * D])
    rq_wb = scr("rq_wb", [D, LD])
    rkv_wb = scr("rkv_wb", [D, 2 * LD])
    rout_wb = scr("rout_wb", [LD, D])
    out_wb = scr("out_wb", [D, D])
    w1T = scr("w1T", [EXP, 128, 8, 8, 128])
    w3T = scr("w3T", [EXP, 128, 8, 8, 128])
    ws1T = scr("ws1T", [128, 16, 8, 128])
    ws3T = scr("ws3T", [128, 16, 8, 128])
    w2T = scr("w2T", [EXP, 128, 8, 1024])
    ws2T = scr("ws2T", [128, 16, 1024])
    ln3_d = scr("ln3_d", [NSLICE, 128, KCH, TSLICE])
    x2_d = nc.dram_tensor("x2_d", [NSLICE, 16, 128, D], F32,
                          kind="Internal").ap()

    dbg_outs = {}

    st = contextlib.ExitStack()
    with st:
        sem_dS = st.enter_context(nc.semaphore("dS"))
        sem_dG = st.enter_context(nc.semaphore("dG"))
        sem_T = st.enter_context(nc.semaphore("semT"))
        sem_V = st.enter_context(nc.semaphore("semV"))
        sem_S = st.enter_context(nc.semaphore("semS"))
        block = st.enter_context(nc.Block())

        QS = Rec(sem_dS, 16)
        QG = Rec(sem_dG, 16)
        T = Rec(sem_T)
        V = Rec(sem_V)
        S = Rec(sem_S)
        _dsems = []

        def mk_dsem(name):
            d = DSem(st.enter_context(nc.semaphore(name)))
            _dsems.append(d)
            return d

        smCast = mk_dsem("smCast")
        smConstA = mk_dsem("smConstA")
        smB2b = mk_dsem("smB2b")
        smBrow = mk_dsem("smBrow")
        smGw = mk_dsem("smGw")
        smRkvw = mk_dsem("smRkvw")
        smWst = [mk_dsem("smWst0"), mk_dsem("smWst1")]
        smTst = [mk_dsem("smTst0"), mk_dsem("smTst1")]
        smW2st = mk_dsem("smW2st")
        smXs = [mk_dsem("smXs0"), mk_dsem("smXs1")]
        smProw2 = [mk_dsem("smProwA"), mk_dsem("smProwB")]
        smObr = mk_dsem("smObr")
        smWsl = mk_dsem("smWsl")
        smGl = mk_dsem("smGl")
        smGlc = mk_dsem("smGlc")
        smX4 = [mk_dsem("smX4_%d" % i) for i in range(4)]
        smSpill = mk_dsem("smSpill")
        smSpillL = mk_dsem("smSpillL")
        smLn3 = mk_dsem("smLn3")
        smFw = [mk_dsem("smFw0"), mk_dsem("smFw1")]
        smW2t = mk_dsem("smW2t")
        smX2l = [mk_dsem("smX2l0"), mk_dsem("smX2l1")]
        smOut = [mk_dsem("smOut0"), mk_dsem("smOut1")]
        smHalo = mk_dsem("smHalo")
        smTap = mk_dsem("smTap")
        ALL = [T, V, S]

        def flush_all():
            QS.flush(block.sync)
            QG.flush(block.gpsimd)
            T.flush(block.tensor)
            V.flush(block.vector)
            S.flush(block.scalar)

        def bar():
            snaps = [(r, r.n) for r in ALL]
            dsnaps = [(d, d.n) for d in _dsems]
            for r in [T, V, S, QS, QG]:
                for r2, v in snaps:
                    if r2 is not r:
                        r.wait(r2, v)
                for d, v in dsnaps:
                    r.wait(d, v)

        _sbctr = [0]

        def sb(name, shape, dt, stack):
            _sbctr[0] += 1
            return stack.enter_context(
                nc.sbuf_tensor(f"{name}_{_sbctr[0]}", list(shape), dt))

        def tap(name, src_fn, shape, dt=F32):
            if not dbgcfg.get(name):
                return
            if name not in dbg_outs:
                dbg_outs[name] = nc.dram_tensor(
                    "dbg_" + name, list(shape), dt, kind="ExternalOutput").ap()
            d = dbg_outs[name]
            bar()
            QG.dma(lambda g: g.dma_start(out=d, in_=src_fn()), smTap)
            bar()

        identb = sb("identb", [128, 128], BF16, st)
        identf_s = sb("identf_s", [128, 128], F32, st)
        ones_b = sb("ones_b", [128, 1], BF16, st)
        onescol_f = sb("onescol_f", [128, 1], F32, st)
        onesrow_f = sb("onesrow_f", [1, 128], F32, st)
        qkvb_c = sb("qkvb_c", [128, 24], F32, st)
        qkvbq_c = sb("qkvbq_c", [128, 8], F32, st)
        rqbq_c = sb("rqbq_c", [128, 2], F32, st)
        rkvb_c = sb("rkvb_c", [128, 4], F32, st)
        routb_c = sb("routb_c", [128, 8], F32, st)
        b1_c = sb("b1_c", [128, 64], F32, st)
        b3_c = sb("b3_c", [128, 64], F32, st)
        bs1_c = sb("bs1_c", [128, 16], F32, st)
        bs3_c = sb("bs3_c", [128, 16], F32, st)
        gateb_c = sb("gateb_c", [8, 1], F32, st)
        bs2_bc = sb("bs2_bc", [128, KCH, 128], F32, st)

        ml_s = sb("ml_s", [NLAT, 32], F32, st)
        gate_wT = sb("gate_wT", [128, 8, 8], BF16, st)

        rk_fm = sb("rk_fm", [128, 2, NLAT], BF16, st)
        rv_tm = sb("rv_tm", [NLAT, 2, 128], BF16, st)
        ktail = sb("ktail", [128, KCH, 128], BF16, st)
        epsc = sb("epsc", [128, 1], F32, st)
        vtail = sb("vtail", [128, 1024], BF16, st)
        obrow2 = sb("obrow2", [1, 2 * D], BF16, st)
        onesrow_b = sb("onesrow_b", [1, 128], BF16, st)

        ps = st.enter_context(nc.psum_tensor("ps", [128, 4096], F32))

        def dummy_mm(bank):
            T.emit(lambda t: t.matmul(
                ps[0:1, bank * 512:bank * 512 + 1], onesrow_b[0:1, 0:1],
                onesrow_b[0:1, 0:1], start=True, stop=True))

        def pb(b0, o=0, n=512):
            return ps[:, 512 * b0 + o: 512 * b0 + o + n]

        class StatCtl:
            def __init__(self, tile, vjk):
                self.tile = tile
                self.vjk = vjk
                self.i = 0
                self.ns = tile.shape[1] // 5
                self.s_done = [0] * self.ns
                self.v_done = [0] * self.ns

            def compute(self, src_fn, jk):
                i = self.i
                self.i = (self.i + 1) % self.ns
                t = self.tile
                b0 = i * 5
                sm = t[:, b0:b0 + 1]
                sq = t[:, b0 + 1:b0 + 2]
                r_ = t[:, b0 + 2:b0 + 3]
                nb = t[:, b0 + 3:b0 + 4]
                tmp = t[:, b0 + 4:b0 + 5]
                V.wait(S, self.s_done[i])
                csm = V.emit(lambda v: v.scalar_tensor_tensor(
                    out=jk, in0=src_fn(), scalar=1.0, in1=src_fn(),
                    op0=ALU.mult, op1=ALU.bypass, accum_out=sm), inc=True)
                c = V.emit(lambda v: v.scalar_tensor_tensor(
                    out=self.vjk, in0=src_fn(), scalar=1.0, in1=src_fn(),
                    op0=ALU.mult, op1=ALU.mult, accum_out=sq), inc=True)
                V.wait(V, c)
                c = V.emit(lambda v: v.tensor_scalar(out=sq, in0=sq,
                                                     scalar1=1.0 / D,
                                                     scalar2=None,
                                                     op0=ALU.mult), inc=True)
                V.wait(V, csm)
                c2m = V.emit(lambda v: v.tensor_scalar(out=sm, in0=sm,
                                                       scalar1=1.0 / D,
                                                       scalar2=None,
                                                       op0=ALU.mult), inc=True)
                V.wait(V, c2m)
                ctm = V.emit(lambda v: v.tensor_tensor(out=tmp, in0=sm, in1=sm,
                                                       op=ALU.mult), inc=True)
                V.wait(V, max(c, ctm))
                c = V.emit(lambda v: v.tensor_tensor(out=sq, in0=sq, in1=tmp,
                                                     op=ALU.subtract), inc=True)
                S.wait(V, c)
                cln = S.emit(lambda s: s.activation(out=tmp, in_=sq, func=AF.Ln,
                                                    bias=epsc[:], scale=1.0),
                             inc=True)
                S.wait(S, cln)
                c2 = S.emit(lambda s: s.activation(out=r_, in_=tmp, func=AF.Exp,
                                                   bias=0.0, scale=-0.5),
                            inc=True)
                V.wait(S, c2)
                cv = V.emit(lambda v: v.tensor_scalar(out=nb, in0=sm,
                                                      scalar1=r_, scalar2=-1.0,
                                                      op0=ALU.mult,
                                                      op1=ALU.mult), inc=True)
                self.v_done[i] = cv
                S.wait(S, c2)
                S.wait(V, cv)
                return i, r_, nb

            def consumed(self, i, s_count):
                self.s_done[i] = s_count

        for (src_, dst_) in [(qkv_w, qkv_wb), (rq_w, rq_wb), (rkv_w, rkv_wb),
                             (rout_w, rout_wb), (out_w, out_wb)]:
            QG.dma(lambda g, s=src_, d=dst_: g.dma_start(out=d[:, :],
                                                         in_=s[:, :]), smCast)
        c_wcast = smCast.n
        QS.dma(lambda q: q.dma_start(out=identf_s[:], in_=identf[:, :]),
               smConstA)
        QS.dma(lambda q: q.dma_start(out=ml_s[:], in_=mld[:, :]), smConstA)
        c_consts = smConstA.n
        V.wait(smConstA, c_consts)
        V.emit(lambda v: v.memset(ones_b[:], 1.0))
        V.emit(lambda v: v.memset(epsc[:], EPS))
        V.emit(lambda v: v.memset(onescol_f[:], 1.0))
        V.emit(lambda v: v.memset(onesrow_f[:], 1.0))
        V.emit(lambda v: v.memset(onesrow_b[:], 1.0))
        V.emit(lambda v: v.tensor_copy(out=identb[:], in_=identf_s[:]), inc=True)
        bar()
        if stop_after == "cast":
            bar()
            flush_all()
            return nc, dbg_outs

        bias_specs = [
            ("qkv_b", qkv_b, 24), ("rq_b", rq_b, 2), ("rkv_b", rkv_b, 4),
            ("rout_b", rout_b, 8), ("bs1", bs1, 16), ("bs3", bs3, 16),
            ("b1", b1.rearrange("e i -> (e i)"), 64),
            ("b3", b3.rearrange("e i -> (e i)"), 64),
        ]
        boff = {}
        off = 0
        for nm, _, n in bias_specs:
            boff[nm] = off
            off += n
        nbch = off

        with contextlib.ExitStack() as past:
            brow = sb("brow", [1, nbch * 128 + 128], F32, past)
            obrow = sb("obrow", [1, 2 * D], F32, past)
            QS.dma(lambda q: q.dma_start(out=obrow[0:1, 0:D],
                                         in_=out_b[None, :]), smObr)
            c_ob = QS.dma(lambda q: q.dma_start(out=obrow[0:1, D:2 * D],
                                                in_=bs2[None, :]), smObr)
            V.wait(smObr, c_ob)
            cvo = V.emit(lambda v: v.tensor_copy(out=obrow2[0:1, 0:D],
                                                 in_=obrow[0:1, 0:D]),
                         inc=True)
            V.wait(V, cvo)
            V.emit(lambda v: v.tensor_tensor(out=obrow2[0:1, D:2 * D],
                                             in0=obrow[0:1, 0:D],
                                             in1=obrow2[0:1, 0:D],
                                             op=ALU.subtract), inc=True)
            gb_off = nbch * 128
            for nm, ap_, n in bias_specs:
                QS.dma(lambda q, a=ap_, o=boff[nm], m=n: q.dma_start(
                    out=brow[0:1, o * 128:(o + m) * 128], in_=a[None, :]),
                    smBrow)
            c_bld = QS.dma(lambda q: q.dma_start(
                out=brow[0:1, gb_off:gb_off + 8], in_=gate_b[None, :]), smBrow)
            T.wait(smBrow, c_bld)
            tgts = [("qkv_b", qkvb_c, 24), ("rkv_b", rkvb_c, 4),
                    ("rout_b", routb_c, 8), ("bs1", bs1_c, 16),
                    ("bs3", bs3_c, 16), ("b1", b1_c, 64), ("b3", b3_c, 64),
                    ("rq_b", rqbq_c, 2)]
            for nm, tgt, n in tgts:
                o = boff[nm]
                for j in range(n):
                    T.emit(lambda t, o=o, j=j: t.matmul(
                        ps[:, 6 * 512 + j:6 * 512 + j + 1],
                        brow[0:1, (o + j) * 128:(o + j + 1) * 128],
                        onesrow_f[0:1, 0:1], start=True, stop=True))
                cT = T.milestone()
                S.wait(T, cT)
                cS = S.emit(lambda s, tgt=tgt, n=n: s.copy(out=tgt[:, 0:n],
                                                           in_=pb(6, 0, n)),
                            inc=True)
                T.wait(S, cS)
            if stop_after == "p0a1":
                flush_all()
                bar()
                flush_all()
                return nc, dbg_outs
            T.emit(lambda t: t.matmul(ps[0:8, 6 * 512:6 * 512 + 1],
                                      brow[0:1, gb_off:gb_off + 8],
                                      onesrow_f[0:1, 0:1], start=True, stop=True))
            cT = T.milestone()
            S.wait(T, cT)
            S.emit(lambda s: s.copy(out=gateb_c[0:8, 0:1],
                                    in_=ps[0:8, 6 * 512:6 * 512 + 1]), inc=True)
            bar()
            bc_ev = [0, 0]
            for k in range(KCH):
                reg = pb(6 + k % 2, 0, 128)
                T.wait(S, bc_ev[k % 2])
                T.emit(lambda t, k=k, reg=reg: t.matmul(
                    reg, onesrow_f[0:1, :],
                    obrow[0:1, D + k * 128:D + (k + 1) * 128],
                    start=True, stop=True))
                cT = T.milestone()
                S.wait(T, cT)
                cS = S.emit(lambda s, k=k, reg=reg: s.copy(out=bs2_bc[:, k, :],
                                                           in_=reg), inc=True)
                bc_ev[k % 2] = cS
            if stop_after == "p0a2":
                flush_all()
                bar()
                flush_all()
                return nc, dbg_outs
            V.wait(S, S.n)
            V.emit(lambda v: v.tensor_scalar(out=qkvbq_c[:], in0=qkvb_c[:, 0:8],
                                             scalar1=0.125, scalar2=None,
                                             op0=ALU.mult))
            V.emit(lambda v: v.tensor_scalar(out=rqbq_c[:], in0=rqbq_c[:],
                                             scalar1=0.125, scalar2=None,
                                             op0=ALU.mult), inc=True)
            bar()
            gwrow = sb("gwrow", [8, 1024], BF16, past)
            cq = QG.dma(lambda g: g.dma_start(out=gwrow[:, :],
                                              in_=gate_w[:, :]), smGw)
            T.wait(smGw, cq)
            for k in range(KCH):
                reg = ps[:, 5 * 512 + (k % 4) * 8:5 * 512 + (k % 4) * 8 + 8]
                T.emit(lambda t, k=k, reg=reg: t.matmul(
                    reg, gwrow[0:8, k * 128:(k + 1) * 128], identb[0:8, 0:8],
                    start=True, stop=True))
                cT = T.milestone()
                S.wait(T, cT)
                cS = S.emit(lambda s, k=k, reg=reg: s.activation(
                    out=gate_wT[:, k, :], in_=reg, func=AF.Identity), inc=True)
                T.wait(S, cS)
            flush_all()
            bar()
            if stop_after == "p0a":
                flush_all()
                return nc, dbg_outs

        with contextlib.ExitStack() as pbst:
            NWB = 4
            wstage = [sb(f"wstage{i}", [128, 1024], BF16, pbst)
                      for i in range(NWB)]
            tstage = [sb(f"tstage{i}", [128, 8, 128], BF16, pbst)
                      for i in range(NWB)]
            w2stage = sb("w2stage", [128, 8, 1024], BF16, pbst)
            smWb = [mk_dsem("smWb%d" % i) for i in range(NWB)]
            smTb = [mk_dsem("smTb%d" % i) for i in range(NWB)]

            xs = [sb(f"xs{i}", [128, 4, 1024], F32, pbst) for i in range(2)]
            jk0 = sb("jk0", [128, 1024], F32, pbst)
            jk0v = sb("jk0v", [128, 1024], F32, pbst)
            ln1w = sb("ln1w", [128, 4, 1024], F32, pbst)
            pooled_tm = sb("pooled_tm", [NLAT, 1024], F32, pbst)
            prow = sb("prow", [1, 2048], F32, pbst)
            pstat_t = sb("pstat_t", [128, 32], F32, pbst)
            pooled_fmb = sb("pooled_fmb", [128, 8, NLAT], BF16, pbst)
            rkvw_sb = sb("rkvw_sb", [128, 8, 512], BF16, pbst)
            rvf = sb("rvf", [128, 2, NLAT], BF16, pbst)
            pstats = StatCtl(pstat_t, jk0v[:])

            QS.wait(smCast, c_wcast)
            c_rkvw = QS.dma(lambda q: q.dma_start(
                out=rkvw_sb[:],
                in_=rkv_wb.rearrange("(a p) b -> p a b", p=128)), smRkvw)

            xf = x_full.rearrange("(n p) d -> n p d", p=128)
            pc_state = {"xs_freed": [0, 0], "pool_ev": [0, 0],
                        "prow_dma": [0, 0], "ln1w_used": 0}

            def pool_stats(blkI):
                par = blkI % 2
                QS.wait(S, pc_state["xs_freed"][par])
                cq = QS.dma(lambda q, b=blkI, p=par: q.dma_start(
                    out=xs[p][:],
                    in_=xf[4 * b:4 * b + 4, :, :].rearrange("a p b -> p a b")),
                    smXs[par])
                V.wait(smXs[par], cq)
                S.wait(smXs[par], cq)
                applies = []
                for tc in range(4):
                    si_, r_, nb_ = pstats.compute(
                        lambda p=par, tc=tc: xs[p][:, tc, :], jk0[:])
                    S.wait(T, pc_state["ln1w_used"] if tc == 0 else 0)
                    cs = S.emit(lambda s, p=par, tc=tc, r_=r_, nb_=nb_:
                                s.activation(out=ln1w[:, tc, :],
                                             in_=xs[p][:, tc, :],
                                             func=AF.Identity, bias=nb_,
                                             scale=r_), inc=True)
                    pstats.consumed(si_, cs)
                    applies.append(cs)
                pc_state["xs_freed"][par] = applies[-1]
                return applies

            def pool_mms(blkI, applies):
                for tc in range(4):
                    chunkid = blkI * 4 + tc
                    ppar = chunkid % 2
                    T.wait(S, applies[tc])
                    T.wait(S, pc_state["pool_ev"][ppar])
                    b0 = 4 + 2 * ppar
                    T.emit(lambda t, b0=b0, tc=tc: t.matmul(
                        ps[0:1, b0 * 512:b0 * 512 + 512],
                        onescol_f[:], ln1w[:, tc, 0:512], start=True,
                        stop=True))
                    T.emit(lambda t, b0=b0, tc=tc: t.matmul(
                        ps[0:1, (b0 + 1) * 512:(b0 + 1) * 512 + 512],
                        onescol_f[:], ln1w[:, tc, 512:1024], start=True,
                        stop=True))
                    cT = T.milestone()
                    pc_state["ln1w_used"] = cT
                    S.wait(T, cT)
                    S.wait(smProw2[ppar], pc_state["prow_dma"][ppar])
                    cs2 = S.emit(lambda s, b0=b0, pp=ppar: s.activation(
                        out=prow[0:1, pp * 1024:pp * 1024 + 1024],
                        in_=ps[0:1, b0 * 512:b0 * 512 + 1024],
                        func=AF.Identity, scale=1.0 / 128.0), inc=True)
                    pc_state["pool_ev"][ppar] = cs2
                    QG.wait(S, cs2)
                    pc_state["prow_dma"][ppar] = QG.dma(
                        lambda g, c=chunkid, pp=ppar: g.dma_start(
                            out=pooled_tm[c:c + 1, :],
                            in_=prow[0:1, pp * 1024:pp * 1024 + 1024]),
                        smProw2[ppar])

            jobs = []
            for e in range(EXP):
                for rb in range(8):
                    jobs.append((w1[e, rb * 128:(rb + 1) * 128, :],
                                 w1T[e, :, rb, :, :]))
                    jobs.append((w3[e, rb * 128:(rb + 1) * 128, :],
                                 w3T[e, :, rb, :, :]))
            for rb in range(16):
                jobs.append((ws1[rb * 128:(rb + 1) * 128, :],
                             ws1T[:, rb, :, :]))
                jobs.append((ws3[rb * 128:(rb + 1) * 128, :],
                             ws3T[:, rb, :, :]))

            load_used = [0] * NWB
            psum_free = [0, 0]
            store_sn = [0] * NWB
            pool_next = [0]

            def prep_job(i, src_, dst_):
                par = i % NWB
                ppair = i % 2
                QG.wait(T, load_used[par])
                cq = QG.dma(lambda g, s=src_, p=par: g.dma_start(
                    out=wstage[p][:], in_=s), smWb[par])
                T.wait(smWb[par], cq)
                T.wait(S, psum_free[ppair])
                for k in range(KCH):
                    T.emit(lambda t, k=k, p=ppair, w=wstage[par]: t.matmul(
                        pb(2 * p + k // 4, (k % 4) * 128, 128),
                        w[:, k * 128:(k + 1) * 128], identb[:],
                        start=True, stop=True))
                cT = T.milestone()
                load_used[par] = cT
                S.wait(T, cT)
                S.wait(smTb[par], store_sn[par])
                cS = S.emit(lambda s, p=par: s.copy(
                    out=tstage[p][:].rearrange("p a b -> p (a b)"),
                    in_=pb(2 * ppair, 0, 1024)), inc=True)
                psum_free[ppair] = cS
                QS.wait(S, cS)
                store_sn[par] = QS.dma(lambda q, d=dst_, p=par: q.dma_start(
                    out=d.rearrange("p a b -> p (a b)"),
                    in_=tstage[p][:].rearrange("p a b -> p (a b)")), smTb[par])

            pend = [None, None]
            for i, (src_, dst_) in enumerate(jobs):
                prep_job(i, src_, dst_)
                if i % 10 == 9 and (pool_next[0] < 16 or pend[0] is not None):
                    if pend[0] is not None:
                        pool_mms(*pend[0])
                    if pool_next[0] < 16:
                        pend[0] = (pool_next[0], pool_stats(pool_next[0]))
                        pool_next[0] += 1
                    else:
                        pend[0] = None
            while pend[0] is not None or pool_next[0] < 16:
                if pend[0] is not None:
                    pool_mms(*pend[0])
                    pend[0] = None
                if pool_next[0] < 16:
                    pend[0] = (pool_next[0], pool_stats(pool_next[0]))
                    pool_next[0] += 1

            w2jobs = [([w2[e, rb * 128:(rb + 1) * 128, :] for rb in range(8)],
                       w2T[e]) for e in range(EXP)]
            w2jobs.append(([ws2[rb * 128:(rb + 1) * 128, 0:1024]
                            for rb in range(8)], ws2T[:, 0:8, :]))
            w2jobs.append(([ws2[rb * 128:(rb + 1) * 128, 1024:2048]
                            for rb in range(8)], ws2T[:, 8:16, :]))
            w2_store = 0
            for srcs, dst_ in w2jobs:
                cS = 0
                for rb in range(8):
                    par = rb % NWB
                    ppair = rb % 2
                    QG.wait(T, load_used[par])
                    cq = QG.dma(lambda g, s=srcs[rb], p=par: g.dma_start(
                        out=wstage[p][:], in_=s), smWb[par])
                    T.wait(smWb[par], cq)
                    T.wait(S, psum_free[ppair])
                    for k in range(KCH):
                        T.emit(lambda t, k=k, p=ppair, w=wstage[par]: t.matmul(
                            pb(2 * p + k // 4, (k % 4) * 128, 128),
                            w[:, k * 128:(k + 1) * 128],
                            identb[:], start=True, stop=True))
                    cT = T.milestone()
                    load_used[par] = cT
                    S.wait(T, cT)
                    if rb < 2:
                        S.wait(smW2st, w2_store)
                    cS = S.emit(lambda s, rb=rb, p=ppair: s.copy(
                        out=w2stage[:, :, rb * 128:(rb + 1) * 128],
                        in_=pb(2 * p, 0, 1024).rearrange("p (a b) -> p a b",
                                                         b=128)), inc=True)
                    psum_free[ppair] = cS
                QS.wait(S, cS)
                w2_store = QS.dma(lambda q, d=dst_: q.dma_start(
                    out=d.rearrange("p a b -> p (a b)"),
                    in_=w2stage[:].rearrange("p a b -> p (a b)")), smW2st)
            flush_all()
            bar()
            if stop_after == "p0b":
                flush_all()
                return nc, dbg_outs
            tap("pooled", lambda: pooled_tm[:, :], [NLAT, 1024])

            for k in range(KCH):
                reg = ps[:, (k % 2) * 512:(k % 2) * 512 + NLAT]
                T.emit(lambda t, k=k, reg=reg: t.matmul(
                    reg, pooled_tm[0:NLAT, k * 128:(k + 1) * 128],
                    identf_s[0:NLAT, 0:NLAT], start=True, stop=True))
                cT = T.milestone()
                S.wait(T, cT)
                cS = S.emit(lambda s, k=k, reg=reg: s.activation(
                    out=pooled_fmb[:, k, :], in_=reg, func=AF.Identity),
                    inc=True)
                if k % 2 == 1:
                    T.wait(S, cS)
            T.wait(smRkvw, c_rkvw)
            rkv_ev = 0
            for m in range(4):
                reg = ps[:, (2 + m % 2) * 512:(2 + m % 2) * 512 + NLAT]
                T.wait(S, rkv_ev)
                for k in range(KCH):
                    T.emit(lambda t, m=m, k=k, reg=reg: t.matmul(
                        reg, rkvw_sb[:, k, m * 128:(m + 1) * 128],
                        pooled_fmb[:, k, :], start=(k == 0),
                        stop=(k == KCH - 1)))
                cT = T.milestone()
                S.wait(T, cT)
                if m < 2:
                    rkv_ev = S.emit(lambda s, m=m, reg=reg: s.activation(
                        out=rk_fm[:, m, :], in_=reg, func=AF.Identity,
                        bias=rkvb_c[:, m:m + 1]), inc=True)
                else:
                    rkv_ev = S.emit(lambda s, m=m, reg=reg: s.activation(
                        out=rvf[:, m - 2, :], in_=reg, func=AF.Identity,
                        bias=rkvb_c[:, m:m + 1]), inc=True)
            S_rv = rkv_ev
            for ch in range(2):
                T.wait(S, S_rv)
                T.emit(lambda t, ch=ch: t.matmul(
                    ps[0:NLAT, 4 * 512:4 * 512 + 128], rvf[:, ch, :],
                    identb[:], start=True, stop=True))
                cT = T.milestone()
                S.wait(T, cT)
                S_rv = S.emit(lambda s, ch=ch: s.activation(
                    out=rv_tm[0:NLAT, ch, :],
                    in_=ps[0:NLAT, 4 * 512:4 * 512 + 128],
                    func=AF.Identity), inc=True)
            flush_all()
            bar()

        xo = x_own.rearrange("(n p) d -> n p d", p=128)
        nslice_run = 0 if stop_after == "p0" else (
            1 if stop_after in ("attn", "slice0") else NSLICE)
        for s_i in range(nslice_run):
            with contextlib.ExitStack() as ast:
                qkv_sb = sb("qkv_sb", [128, KCH, 3 * D], BF16, ast)
                outw_sb = sb("outw_sb", [128, KCH, D], BF16, ast)
                rqw_sb = sb("rqw_sb", [128, KCH, LD], BF16, ast)
                routw_sb = sb("routw_sb", [128, 2, D], BF16, ast)
                xblk = sb("xblk", [128, 8, 1024], F32, ast)
                lnN = [ktail[:].rearrange("p a b -> p (a b)"), vtail[:]]
                lntm = lnN[0]
                ln1f = sb("ln1f", [128, KCH, BT], BF16, ast)
                qf = sb("qf", [128, KCH, BT], BF16, ast)
                kbuf = sb("kbuf", [128, KCH, 640], BF16, ast)
                vbuf = sb("vbuf", [128, 5, 1024], BF16, ast)
                rqf = sb("rqf", [128, 2, BT], BF16, ast)
                Ebig = sb("Ebig", [128, H, 256], BF16, ast)

                loc_tm = sb("loc_tm", [128, H, HDIM], BF16, ast)
                lat_tm = sb("lat_tm", [128, LH, HDIM], BF16, ast)
                loc_f = sb("loc_f", [128, KCH, BT], BF16, ast)
                lat_f = sb("lat_f", [128, 2, BT], BF16, ast)
                rem_f = sb("rem_f", [128, KCH, BT], BF16, ast)
                mixf = sb("mixf", [128, KCH, BT], BF16, ast)

                ao_t = sb("ao_t", [128, 1024], F32, ast)
                jk = sb("jk", [128, 1024], F32, ast)
                astat_t = sb("astat_t", [128, 32], F32, ast)
                rs_c = sb("rs_c", [128, 8], F32, ast)
                maskT_s = sb("maskT_s", [128, 2, 256], BF16, ast)
                glmat = sb("glmat", [128, KCH, 16], BF16, ast)
                grmat = sb("grmat", [128, KCH, 16], BF16, ast)
                ehm_s = sb("ehm_s", [16, 8, 128], BF16, ast)
                glcol = sb("glcol", [64, 2], BF16, ast)
                gsigb = jk[0:16, 512:768].bitcast(BF16)
                Etmp = jk[:, 768:896].bitcast(BF16)
                Etmp2 = jk[:, 896:1024].bitcast(BF16)
                Elat = ao_t[0:64, 512:768].bitcast(BF16).rearrange(
                    "p (a b) -> p a b", b=128)
                jkv = Ebig[:].rearrange("p a b -> p (a b)").bitcast(F32)
                astats = StatCtl(astat_t, jkv[:, 0:1024])
                sjk = jkv[:, 1024:2048]
                gsig = ao_t[0:16, 0:512]
                ln23 = jk

                QS.wait(smCast, c_wcast)
                QS.dma(lambda q: q.dma_start(
                    out=qkv_sb[:],
                    in_=qkv_wb.rearrange("(a p) b -> p a b", p=128)), smWsl)
                QS.dma(lambda q: q.dma_start(
                    out=outw_sb[:],
                    in_=out_wb.rearrange("(a p) b -> p a b", p=128)), smWsl)
                QS.dma(lambda q: q.dma_start(
                    out=rqw_sb[:],
                    in_=rq_wb.rearrange("(a p) b -> p a b", p=128)), smWsl)
                c_wslice = QS.dma(lambda q: q.dma_start(
                    out=routw_sb[:],
                    in_=rout_wb.rearrange("(a p) b -> p a b", p=128)), smWsl)
                QG.dma(lambda g: g.dma_start(
                    out=maskT_s[:],
                    in_=maskTd[:, :, :].rearrange("a p b -> p a b")), smGl)
                QG.dma(lambda g: g.dma_start(
                    out=ehm_s[:].rearrange("p a b -> p (a b)"),
                    in_=ehmatd[:, :, :].rearrange("p a b -> p (a b)")), smGl)
                QG.dma(lambda g: g.dma_start(out=glcol[0:64, 0:1],
                                             in_=gl_w[:, :]), smGlc)
                cg = QG.dma(lambda g: g.dma_start(out=glcol[0:64, 1:2],
                                                  in_=gr_w[:, :]), smGlc)
                V.wait(smGlc, cg)
                V.emit(lambda v: v.memset(
                    glmat[:].rearrange("p a b -> p (a b)"), 0.0))
                cvz = V.emit(lambda v: v.memset(
                    grmat[:].rearrange("p a b -> p (a b)"), 0.0), inc=True)
                QG.wait(V, cvz)
                QG.wait(smGlc, cg)
                for c in range(KCH):
                    QG.dma(lambda g, c=c: g.dma_start(
                        out=glmat[0:64, c, 2 * c:2 * c + 1],
                        in_=glcol[0:64, 0:1]), smGl)
                    QG.dma(lambda g, c=c: g.dma_start(
                        out=glmat[64:128, c, 2 * c + 1:2 * c + 2],
                        in_=glcol[0:64, 0:1]), smGl)
                    QG.dma(lambda g, c=c: g.dma_start(
                        out=grmat[0:64, c, 2 * c:2 * c + 1],
                        in_=glcol[0:64, 1:2]), smGl)
                    QG.dma(lambda g, c=c: g.dma_start(
                        out=grmat[64:128, c, 2 * c + 1:2 * c + 2],
                        in_=glcol[0:64, 1:2]), smGl)
                flush_all()
                bar()

                if s_i == 0:
                    cq = QS.dma(lambda q: q.dma_start(out=ao_t[:],
                                                      in_=x_halo[:, :]),
                                smHalo)
                    V.wait(smHalo, cq)
                    S.wait(smHalo, cq)
                    si_, r_, nb_ = astats.compute(lambda: ao_t[:], sjk)
                    ch0 = S.emit(lambda s, r_=r_, nb_=nb_: s.activation(
                        out=lntm[:], in_=ao_t[:], func=AF.Identity,
                        bias=nb_, scale=r_), inc=True)
                    astats.consumed(si_, ch0)
                    T.wait(S, ch0)
                    T.wait(smWsl, c_wslice)
                    for k in range(KCH):
                        T.emit(lambda t, k=k: t.matmul(
                            pb(k // 4, (k % 4) * 128, 128),
                            lntm[:, k * 128:(k + 1) * 128], identb[:],
                            start=True, stop=True))
                    cT = T.milestone()
                    V.wait(T, cT)
                    cV = V.emit(lambda v: v.tensor_copy(
                        out=ln1f[:, :, 0:128],
                        in_=pb(0, 0, 1024).rearrange("p (a b) -> p a b", b=128)),
                        inc=True)
                    T.wait(V, cV)
                    ev = 0
                    for m in range(KCH):
                        reg = pb(2 + m % 2, 0, 128)
                        T.wait(S, ev)
                        for k in range(KCH):
                            T.emit(lambda t, m=m, k=k, reg=reg: t.matmul(
                                reg, qkv_sb[:, k, D + m * 128:D + (m + 1) * 128],
                                ln1f[:, k, 0:128], start=(k == 0),
                                stop=(k == KCH - 1)))
                        cT = T.milestone()
                        S.wait(T, cT)
                        ev = S.emit(lambda s, m=m, reg=reg: s.activation(
                            out=ktail[:, m, :], in_=reg, func=AF.Identity,
                            bias=qkvb_c[:, 8 + m:9 + m]), inc=True)
                    T.wait(S, ev)
                    for half in range(2):
                        for k in range(KCH):
                            T.emit(lambda t, k=k, half=half: t.matmul(
                                pb(4 + half, 0, 512), ln1f[:, k, 0:128],
                                qkv_sb[:, k, 2 * D + half * 512:
                                       2 * D + (half + 1) * 512],
                                start=(k == 0), stop=(k == KCH - 1)))
                    cT = T.milestone()
                    S.wait(T, cT)
                    S.emit(lambda s: s.copy(out=vtail[:], in_=pb(4, 0, 1024)),
                           inc=True)
                    flush_all()
                    bar()

                cb = {"attnT": 0,
                      "rm_ev": [0, 0],
                      "og_ev": [0, 0],
                      "stS_ev": [0, 0],
                      "av_sev": [0, 0],
                      "ltT_evS": 0, "ltT_evV": 0,
                      "avT": [0] * H,
                      "latT": [0] * LH,
                      "prev_mask": 0, "locT": 0, "lnG2": 0,
                      "ln3sP": [[0] * 4, [0] * 4],
                      "spxP": [0, 0],
                      "v_ev": [0, 0],
                      "qk_ev": [0, 0],
                      "carry": 0,
                      "lnN_guard": [0, 0],
                      "lnN_evV": 0,
                      "ln3sp": 0,
                      "lnT_evV": 0,
                      "acons": 0,
                      "xadd": 0}

                def emit_carry_A(blk):
                    par = blk % 2
                    cq_xs = []
                    for tc in range(4):
                        QS.wait(S, cb["ln3sP"][par][tc])
                        if tc == 0:
                            QS.wait(smSpill, cb["spxP"][par])
                            QS.wait(S, cb["acons"])
                        cq_xs.append(QS.dma(lambda q, b=blk, s=s_i, par=par,
                                            tc=tc: q.dma_start(
                            out=xblk[:, 4 * par + tc, :],
                            in_=xo[16 * s + 4 * b + tc, :, :]), smX4[tc]))
                    V.wait(T, cb["attnT"])
                    V.wait(S, cb["qk_ev"][0])
                    V.wait(S, cb["qk_ev"][1])
                    V.wait(S, cb["v_ev"][0])
                    V.wait(S, cb["v_ev"][1])
                    if blk == 0:
                        V.emit(lambda v: v.tensor_copy(
                            out=kbuf[:, :, 0:128], in_=ktail[:]))
                        cV_carry = V.emit(lambda v: v.tensor_copy(
                            out=vbuf[:, 0, :], in_=vtail[:]), inc=True)
                    else:
                        V.emit(lambda v: v.tensor_copy(
                            out=kbuf[:, :, 0:128], in_=kbuf[:, :, 512:640]))
                        cV_carry = V.emit(lambda v: v.tensor_copy(
                            out=vbuf[:, 0, :], in_=vbuf[:, 4, :]), inc=True)
                    cb["carry"] = cV_carry
                    S.wait(T, cb["attnT"])
                    st_res = []
                    for tc in range(4):
                        V.wait(smX4[tc], cq_xs[tc])
                        S.wait(smX4[tc], cq_xs[tc])
                        st_res.append(astats.compute(
                            lambda par=par, tc=tc: xblk[:, 4 * par + tc, :],
                            sjk))
                    ln_gv = cb["lnN_evV"]
                    for tc in range(4):
                        si_, r_, nb_ = st_res[tc]
                        S.wait(T, cb["lnN_guard"][tc % 2])
                        if tc < 2:
                            S.wait(V, cb["carry"])
                        cs = S.emit(lambda s, par=par, tc=tc, r_=r_, nb_=nb_:
                                    s.activation(out=lnN[tc % 2],
                                                 in_=xblk[:, 4 * par + tc, :],
                                                 func=AF.Identity, bias=nb_,
                                                 scale=r_), inc=True)
                        astats.consumed(si_, cs)
                        T.wait(S, cs)
                        T.wait(V, ln_gv)
                        if tc == 0:
                            T.wait(S, cb["av_sev"][0])
                            T.wait(S, cb["av_sev"][1])
                            T.wait(S, cb["v_ev"][0])
                            T.wait(S, cb["v_ev"][1])
                        if tc == 1:
                            T.wait(S, cb["ltT_evS"])
                        b0 = 2 * (tc % 2)
                        for k in range(KCH):
                            T.emit(lambda t, k=k, b0=b0, tc=tc: t.matmul(
                                pb(b0 + k // 4, (k % 4) * 128, 128),
                                lnN[tc % 2][:, k * 128:(k + 1) * 128],
                                identb[:], start=True, stop=True))
                        cT = T.milestone()
                        cb["lnN_guard"][tc % 2] = cT
                        V.wait(T, cT)
                        cV = V.emit(lambda v, tc=tc, b0=b0: v.tensor_copy(
                            out=ln1f[:, :, tc * 128:(tc + 1) * 128],
                            in_=pb(b0, 0, 1024).rearrange("p (a b) -> p a b",
                                                          b=128)), inc=True)
                        ln_gv = cV
                    cb["lnN_evV"] = ln_gv
                    cb["acons"] = cs

                def emit_B(blk):
                    T.wait(V, cb["lnN_evV"])
                    qk_ev = [cb["rm_ev"][0], cb["rm_ev"][1]]
                    for m in range(18):
                        par = m % 2
                        reg = pb(4 + par)
                        T.wait(S, qk_ev[par])
                        if m < 2:
                            S.wait(T, cb["attnT"])
                            S.wait(V, cb["carry"])
                        for k in range(KCH):
                            if m < 16:
                                T.emit(lambda t, m=m, k=k, reg=reg: t.matmul(
                                    reg, qkv_sb[:, k, m * 128:(m + 1) * 128],
                                    ln1f[:, k, :], start=(k == 0),
                                    stop=(k == KCH - 1)))
                            else:
                                T.emit(lambda t, m=m, k=k, reg=reg: t.matmul(
                                    reg, rqw_sb[:, k,
                                                (m - 16) * 128:(m - 15) * 128],
                                    ln1f[:, k, :], start=(k == 0),
                                    stop=(k == KCH - 1)))
                        cT = T.milestone()
                        S.wait(T, cT)
                        if m < 8:
                            cS = S.emit(lambda s, m=m, reg=reg: s.activation(
                                out=qf[:, m, :], in_=reg, func=AF.Identity,
                                scale=0.125, bias=qkvbq_c[:, m:m + 1]),
                                inc=True)
                        elif m < 16:
                            cS = S.emit(lambda s, m=m, reg=reg: s.activation(
                                out=kbuf[:, m - 8, 128:640], in_=reg,
                                func=AF.Identity, bias=qkvb_c[:, m:m + 1]),
                                inc=True)
                        else:
                            cS = S.emit(lambda s, m=m, reg=reg: s.activation(
                                out=rqf[:, m - 16, :], in_=reg,
                                func=AF.Identity, scale=0.125,
                                bias=rqbq_c[:, m - 16:m - 15]), inc=True)
                        qk_ev[par] = cS
                    cb["qk_ev"] = qk_ev

                def emit_C(blk, og_ev):
                    v_ev = [cb["v_ev"][0], cb["v_ev"][1]]
                    for tc in range(4):
                        par = tc % 2
                        T.wait(S, v_ev[par])
                        if tc < 2:
                            T.wait(V, og_ev[par])
                        for half in range(2):
                            for k in range(KCH):
                                T.emit(lambda t, tc=tc, k=k, half=half,
                                       par=par: t.matmul(
                                    pb(2 * par + half, 0, 512),
                                    ln1f[:, k, tc * 128:(tc + 1) * 128],
                                    qkv_sb[:, k, 2 * D + half * 512:
                                           2 * D + (half + 1) * 512],
                                    start=(k == 0), stop=(k == KCH - 1)))
                        cT = T.milestone()
                        S.wait(T, cT)
                        cS = S.emit(lambda s, tc=tc, par=par: s.copy(
                            out=vbuf[:, 1 + tc, :], in_=pb(2 * par, 0, 1024)),
                            inc=True)
                        v_ev[par] = cS
                    cb["v_ev"] = v_ev

                emit_carry_A(0)
                emit_B(0)
                emit_C(0, [0, 0])
                flush_all()

                for b_i in range(NBLK):
                    stS_ev = cb["stS_ev"]
                    av_sev = cb["av_sev"]
                    av_vev = [0, 0]
                    T.wait(S, cb["qk_ev"][0])
                    T.wait(S, cb["qk_ev"][1])
                    T.wait(S, cb["v_ev"][0])
                    T.wait(S, cb["v_ev"][1])
                    T.wait(V, cb["carry"])
                    T.wait(V, cb["lnT_evV"])
                    S.wait(smSpillL, cb["ln3sp"])
                    stS_ev[0] = max(stS_ev[0], 0)
                    prev_scnt = cb.get("scnt3", [])
                    for j in range(NSUB):
                        scnt_cur = []
                        if j == 0:
                            T.wait(S, max(cb["qk_ev"][0], cb["qk_ev"][1]))
                            for c_ in cb.get("gcnts", [])[-8:]:
                                T.wait(S, c_)
                                dummy_mm(5)
                        else:
                            for c_ in prev_scnt[-12::2]:
                                T.wait(S, c_)
                                dummy_mm(5)
                        prev_scnt = scnt_cur
                        c_loc = s_i * 16 + b_i * 4 + j
                        first = (s_i == 0 and b_i == 0 and j == 0)
                        msel = 1 if first else 0
                        mask_cnt = [0] * H

                        def st_mms(h):
                            ch = h // 2
                            o64 = 64 * (h % 2)
                            r2 = h % 2
                            T.wait(S, stS_ev[r2])
                            for half in range(2):
                                T.emit(lambda t, h=h, half=half, j=j, ch=ch,
                                       o64=o64, r2=r2: t.matmul(
                                    ps[:, (6 + r2) * 512 + half * 128:
                                       (6 + r2) * 512 + half * 128 + 128],
                                    kbuf[o64:o64 + 64, ch,
                                         j * 128 + half * 128:
                                         j * 128 + half * 128 + 128],
                                    qf[o64:o64 + 64, ch,
                                       j * 128:(j + 1) * 128],
                                    start=True, stop=True))
                            return T.milestone()

                        def av_mms(h):
                            r4 = h % 2
                            reg = ps[:, r4 * 512:r4 * 512 + 72]
                            T.wait(S, av_sev[r4])
                            T.wait(V, mask_cnt[h])
                            if j == 0 and h < 2:
                                T.wait(V, cb["og_ev"][r4])
                            for half in range(2):
                                T.emit(lambda t, h=h, j=j, half=half, reg=reg:
                                       t.matmul(
                                    reg[:, 0:64],
                                    Ebig[:, h, half * 128:(half + 1) * 128],
                                    vbuf[:, j + half, h * 64:(h + 1) * 64],
                                    start=(half == 0), stop=(half == 1)))
                            for half in range(2):
                                T.emit(lambda t, h=h, half=half, reg=reg:
                                       t.matmul(
                                    reg[:, 64:65],
                                    Ebig[:, h, half * 128:(half + 1) * 128],
                                    ones_b[:], start=(half == 0),
                                    stop=(half == 1)))
                            cT = T.milestone()
                            cb["avT"][h] = cT
                            V.wait(T, cT)
                            V.wait(S, av_sev[r4])
                            c0 = V.emit(lambda v, reg=reg, r4=r4:
                                        v.tensor_scalar(
                                out=rs_c[:, r4:r4 + 1], in0=reg[:, 64:65],
                                scalar1=1e-30, scalar2=None, op0=ALU.max),
                                inc=True)
                            V.wait(V, c0)
                            cV = V.emit(lambda v, r4=r4: v.reciprocal(
                                out=rs_c[:, r4:r4 + 1],
                                in_=rs_c[:, r4:r4 + 1]), inc=True)
                            av_vev[r4] = cV
                            S.wait(V, cV)
                            S.wait(T, cb["locT"])
                            cS = S.emit(lambda s, h=h, reg=reg, r4=r4:
                                        s.activation(
                                out=loc_tm[:, h, :], in_=reg[:, 0:64],
                                func=AF.Identity,
                                scale=rs_c[:, r4:r4 + 1]), inc=True)
                            av_sev[r4] = cS
                            scnt_cur.append(cS)

                        for h in range(H):
                            cT = st_mms(h)
                            S.wait(T, cT)
                            S.wait(V, mask_cnt[h - 2] if h >= 2
                                   else cb["prev_mask"])
                            et = Etmp[:] if h % 2 == 0 else Etmp2[:]
                            cSe = S.emit(lambda s, r2=h % 2, et=et:
                                         s.activation(
                                out=et,
                                in_=ps[:, (6 + r2) * 512:(6 + r2) * 512 + 256],
                                func=AF.Exp), inc=True)
                            stS_ev[h % 2] = cSe
                            scnt_cur.append(cSe)
                            V.wait(S, cSe)
                            V.wait(T, cb["avT"][h])
                            cVm = V.emit(lambda v, h=h, msel=msel, et=et:
                                         v.tensor_tensor(
                                out=Ebig[:, h, :], in0=et,
                                in1=maskT_s[:, msel, :], op=ALU.mult),
                                inc=True)
                            mask_cnt[h] = cVm
                            if h >= 1:
                                av_mms(h - 1)
                        av_mms(H - 1)

                        for lh in range(LH):
                            chl = lh // 2
                            o64 = 64 * (lh % 2)
                            rl = lh % 2
                            T.wait(S, stS_ev[rl])
                            T.emit(lambda t, lh=lh, j=j, chl=chl, o64=o64,
                                   rl=rl: t.matmul(
                                ps[0:NLAT, (6 + rl) * 512:(6 + rl) * 512 + 128],
                                rk_fm[o64:o64 + 64, chl, :],
                                rqf[o64:o64 + 64, chl, j * 128:(j + 1) * 128],
                                start=True, stop=True))
                            cT = T.milestone()
                            S.wait(T, cT)
                            cSe = S.emit(lambda s, lh=lh, rl=rl, c=c_loc:
                                         s.activation(
                                out=Elat[0:NLAT, lh, :],
                                in_=ps[0:NLAT, (6 + rl) * 512:
                                       (6 + rl) * 512 + 128],
                                func=AF.Exp, bias=ml_s[0:NLAT, c:c + 1]),
                                inc=True)
                            stS_ev[rl] = cSe
                            scnt_cur.append(cSe)
                            r4 = lh % 2
                            reg = ps[:, r4 * 512:r4 * 512 + 72]
                            T.wait(S, av_sev[r4])
                            T.wait(S, cSe)
                            T.emit(lambda t, lh=lh, reg=reg, chl=chl, o64=o64:
                                   t.matmul(
                                reg[:, 0:64], Elat[0:NLAT, lh, :],
                                rv_tm[0:NLAT, chl, o64:o64 + 64],
                                start=True, stop=True))
                            T.emit(lambda t, lh=lh, reg=reg: t.matmul(
                                reg[:, 64:65], Elat[0:NLAT, lh, :],
                                ones_b[0:NLAT, :], start=True, stop=True))
                            cT = T.milestone()
                            V.wait(T, cT)
                            V.wait(S, av_sev[r4])
                            c0 = V.emit(lambda v, reg=reg, r4=r4:
                                        v.tensor_scalar(
                                out=rs_c[:, 4 + r4:5 + r4], in0=reg[:, 64:65],
                                scalar1=1e-30, scalar2=None, op0=ALU.max),
                                inc=True)
                            V.wait(V, c0)
                            cV = V.emit(lambda v, r4=r4: v.reciprocal(
                                out=rs_c[:, 4 + r4:5 + r4],
                                in_=rs_c[:, 4 + r4:5 + r4]), inc=True)
                            S.wait(V, cV)
                            S.wait(T, cb["locT"])
                            cS = S.emit(lambda s, lh=lh, reg=reg, r4=r4:
                                        s.activation(
                                out=lat_tm[:, lh, :], in_=reg[:, 0:64],
                                func=AF.Identity,
                                scale=rs_c[:, 4 + r4:5 + r4]), inc=True)
                            av_sev[r4] = cS
                            scnt_cur.append(cS)
                        cb["prev_mask"] = mask_cnt[H - 1]
                        flush_all()

                        T.wait(S, av_sev[0])
                        T.wait(S, av_sev[1])
                        T.wait(S, cb["ltT_evS"])
                        T.wait(V, cb["ltT_evV"])
                        for k in range(KCH):
                            T.emit(lambda t, k=k: t.matmul(
                                pb(2 + k // 4, (k % 4) * 128, 128),
                                loc_tm[:].rearrange("p a b -> p (a b)")[
                                    :, k * 128:(k + 1) * 128],
                                identb[:], start=True, stop=True))
                        for ch2 in range(2):
                            T.emit(lambda t, ch2=ch2: t.matmul(
                                pb(4, ch2 * 128, 128),
                                lat_tm[:].rearrange("p a b -> p (a b)")[
                                    :, ch2 * 128:(ch2 + 1) * 128],
                                identb[:], start=True, stop=True))
                        cT = T.milestone()
                        S.wait(T, cT)
                        for k in range(KCH):
                            S.emit(lambda s, k=k, j=j: s.activation(
                                out=loc_f[:, k, j * 128:(j + 1) * 128],
                                in_=pb(2 + k // 4, (k % 4) * 128, 128),
                                func=AF.Identity,
                                bias=qkvb_c[:, 16 + k:17 + k]), inc=True)
                        cb["locT"] = cT
                        V.wait(T, cT)
                        for ch2 in range(2):
                            cVl = V.emit(lambda v, ch2=ch2, j=j: v.tensor_copy(
                                out=lat_f[:, ch2, j * 128:(j + 1) * 128],
                                in_=pb(4, ch2 * 128, 128)), inc=True)
                        cb["ltT_evS"] = cS
                        cb["ltT_evV"] = cVl
                        flush_all()
                    cb["scnt3"] = list(prev_scnt)
                    for c_ in prev_scnt[-14::2]:
                        T.wait(S, c_)
                        dummy_mm(5)
                    cb["attnT"] = T.n
                    tap("loc_f_%d_%d" % (s_i, b_i),
                        lambda: loc_f[:].rearrange("p a b -> p (a b)"),
                        [128, KCH * BT], BF16)
                    tap("lat_f_%d_%d" % (s_i, b_i),
                        lambda: lat_f[:].rearrange("p a b -> p (a b)"),
                        [128, 2 * BT], BF16)

                    rm_ev = [cb["qk_ev"][0], cb["qk_ev"][1]]
                    T.wait(V, cb["ltT_evV"])
                    for m in range(KCH):
                        par = m % 2
                        reg = pb(4 + par)
                        T.wait(S, rm_ev[par])
                        for k in range(2):
                            T.emit(lambda t, m=m, k=k, reg=reg: t.matmul(
                                reg, routw_sb[:, k, m * 128:(m + 1) * 128],
                                lat_f[:, k, :], start=(k == 0), stop=(k == 1)))
                        cT = T.milestone()
                        S.wait(T, cT)
                        cS = S.emit(lambda s, m=m, reg=reg: s.activation(
                            out=rem_f[:, m, :], in_=reg, func=AF.Identity,
                            bias=routb_c[:, m:m + 1]), inc=True)
                        rm_ev[par] = cS
                    cb["rm_ev"] = rm_ev
                    flush_all()
                    tap("rem_f_%d_%d" % (s_i, b_i),
                        lambda: rem_f[:].rearrange("p a b -> p (a b)"),
                        [128, KCH * BT], BF16)

                    if b_i + 1 < NBLK:
                        emit_carry_A(b_i + 1)

                    T.wait(S, cb["ltT_evS"])
                    T.wait(S, rm_ev[0])
                    T.wait(S, rm_ev[1])
                    T.wait(S, stS_ev[0])
                    T.wait(S, stS_ev[1])
                    for k in range(KCH):
                        T.emit(lambda t, k=k: t.matmul(
                            ps[0:16, 6 * 512:6 * 512 + 512], glmat[:, k, :],
                            loc_f[:, k, :], start=(k == 0), stop=False))
                    for k in range(KCH):
                        T.emit(lambda t, k=k: t.matmul(
                            ps[0:16, 6 * 512:6 * 512 + 512], grmat[:, k, :],
                            rem_f[:, k, :], start=False, stop=(k == KCH - 1)))
                    cT = T.milestone()
                    S.wait(T, cT)
                    cS = S.emit(lambda s: s.activation(
                        out=gsig[0:16, :], in_=ps[0:16, 6 * 512:6 * 512 + 512],
                        func=AF.Exp, scale=-1.0), inc=True)
                    V.wait(S, cS)
                    c0 = V.emit(lambda v: v.tensor_scalar(
                        out=gsig[0:16, :], in0=gsig[0:16, :], scalar1=1.0,
                        scalar2=None, op0=ALU.add), inc=True)
                    V.wait(V, c0)
                    c0 = V.emit(lambda v: v.reciprocal(out=gsig[0:16, :],
                                                       in_=gsig[0:16, :]),
                                inc=True)
                    V.wait(V, c0)
                    cVg = V.emit(lambda v: v.tensor_copy(out=gsigb[0:16, :],
                                                         in_=gsig[0:16, :]),
                                 inc=True)
                    gb_ev = [cVg, cVg]
                    if b_i + 1 < NBLK:
                        emit_B(b_i + 1)
                    prev_mix = 0
                    for c in range(KCH):
                        gpar = c % 2
                        T.wait(V, gb_ev[gpar])
                        T.emit(lambda t, c=c, gpar=gpar: t.matmul(
                            pb(6 + gpar), ehm_s[:, c, :], gsigb[0:16, :],
                            start=True, stop=True))
                        cT = T.milestone()
                        V.wait(T, cT)
                        V.wait(V, prev_mix)
                        c0 = V.emit(lambda v, c=c: v.tensor_tensor(
                            out=jk[:, 0:512], in0=loc_f[:, c, :],
                            in1=rem_f[:, c, :], op=ALU.subtract), inc=True)
                        V.wait(V, c0)
                        c0 = V.emit(lambda v, c=c, gpar=gpar: v.tensor_tensor(
                            out=jk[:, 0:512], in0=jk[:, 0:512], in1=pb(6 + gpar),
                            op=ALU.mult), inc=True)
                        V.wait(V, c0)
                        gb_ev[gpar] = V.emit(lambda v, c=c: v.tensor_tensor(
                            out=mixf[:, c, :], in0=jk[:, 0:512],
                            in1=rem_f[:, c, :], op=ALU.add), inc=True)
                        prev_mix = gb_ev[gpar]
                    flush_all()
                    tap("mixf_%d_%d" % (s_i, b_i),
                        lambda: mixf[:].rearrange("p a b -> p (a b)"),
                        [128, KCH * BT], BF16)

                    og_ev = [0, 0]
                    ao_guard = 0
                    par_b = b_i % 2
                    T.wait(V, prev_mix)
                    T.wait(V, cb["lnN_evV"])
                    ln3_apply = [0, 0, 0, 0]
                    lnT_cT = [0, 0, 0, 0]
                    gb_guard = prev_mix
                    c_spx = 0
                    lnst = {"sp": 0, "evV": cb["lnT_evV"]}
                    gcnts = []
                    ltmf = loc_tm[:].rearrange("p a b -> p (a b)")

                    def emit_lnT(tc):
                        T.wait(S, ln3_apply[tc])
                        if tc == 0:
                            T.wait(V, gb_guard)
                        else:
                            T.wait(V, lnst["evV"])
                        for k in range(KCH):
                            T.emit(lambda t, k=k, tc=tc: t.matmul(
                                pb(6 + k // 4, (k % 4) * 128, 128),
                                lnN[tc % 2][:, k * 128:(k + 1) * 128],
                                identb[:], start=True, stop=True))
                        cT = T.milestone()
                        lnT_cT[tc] = cT
                        V.wait(T, cT)
                        V.wait(smSpillL, lnst["sp"])
                        lnst["evV"] = V.emit(lambda v: v.tensor_copy(
                            out=ltmf[:, :], in_=pb(6, 0, 1024)), inc=True)
                        QG.wait(V, lnst["evV"])
                        lnst["sp"] = QG.dma(lambda g, s=s_i, b=b_i, tc=tc:
                                            g.dma_start(
                            out=ln3_d[s, :, :,
                                      b * 512 + tc * 128:
                                      b * 512 + (tc + 1) * 128],
                            in_=ltmf[:, :].rearrange("p (a b) -> p a b",
                                                     b=128)), smSpillL)

                    for tc in range(4):
                        par = tc % 2
                        slot = 4 * par_b + tc
                        T.wait(V, og_ev[par])
                        for half in range(2):
                            T.emit(lambda t, half=half, par=par: t.matmul(
                                pb(2 * par + half, 0, 512),
                                onesrow_b[0:1, :],
                                obrow2[0:1, half * 512:(half + 1) * 512],
                                start=True, stop=False))
                            T.emit(lambda t, half=half, par=par: t.matmul(
                                pb(2 * par + half, 0, 512),
                                onesrow_b[0:1, :],
                                obrow2[0:1, D + half * 512:
                                       D + (half + 1) * 512],
                                start=False, stop=False))
                            for k in range(KCH):
                                T.emit(lambda t, tc=tc, k=k, half=half,
                                       par=par: t.matmul(
                                    pb(2 * par + half, 0, 512),
                                    mixf[:, k, tc * 128:(tc + 1) * 128],
                                    outw_sb[:, k, half * 512:(half + 1) * 512],
                                    start=False, stop=(k == KCH - 1)))
                        cT = T.milestone()
                        if tc >= 2:
                            emit_lnT(tc - 2)
                        V.wait(T, cT)
                        V.wait(S, ao_guard)
                        cV = V.emit(lambda v, par=par: v.tensor_copy(
                            out=ao_t[:], in_=pb(2 * par, 0, 1024)), inc=True)
                        og_ev[par] = cV
                        S.wait(V, cV)
                        V.wait(V, cV)
                        si_, r_, nb_ = astats.compute(lambda: ao_t[:], sjk)
                        cs = S.emit(lambda s, r_=r_, nb_=nb_: s.activation(
                            out=ln23[:, 0:1024], in_=ao_t[:],
                            func=AF.Identity, bias=nb_, scale=r_), inc=True)
                        astats.consumed(si_, cs)
                        ao_guard = cs
                        gcnts.append(cs)
                        V.wait(S, cs)
                        cV = V.emit(lambda v, slot=slot: v.tensor_tensor(
                            out=xblk[:, slot, :], in0=xblk[:, slot, :],
                            in1=ln23[:, 0:1024], op=ALU.add), inc=True)
                        QG.wait(V, cV)
                        c_spx = QG.dma(lambda g, s=s_i, b=b_i, slot=slot:
                                       g.dma_start(
                            out=x2_d[s, 4 * b + slot - 4 * (b % 2)],
                            in_=xblk[:, slot, :]), smSpill)
                        S.wait(V, cV)
                        V.wait(V, cV)
                        si_, r_, nb_ = astats.compute(
                            lambda slot=slot: xblk[:, slot, :], sjk)
                        if tc <= 1:
                            S.wait(T, cb["lnN_guard"][tc % 2])
                        else:
                            S.wait(T, lnT_cT[tc - 2])
                        cs = S.emit(lambda s, slot=slot, tc=tc, r_=r_, nb_=nb_:
                                    s.activation(
                                        out=lnN[tc % 2],
                                        in_=xblk[:, slot, :],
                                        func=AF.Identity, bias=nb_,
                                        scale=r_), inc=True)
                        astats.consumed(si_, cs)
                        ln3_apply[tc] = cs
                        gcnts.append(cs)
                    cb["gcnts"] = gcnts
                    cb["og_ev"] = og_ev
                    cb["xadd"] = cV
                    cb["ln3sP"][par_b] = list(ln3_apply)
                    cb["spxP"][par_b] = c_spx

                    if b_i + 1 < NBLK:
                        emit_C(b_i + 1, og_ev)

                    emit_lnT(2)
                    emit_lnT(3)
                    cb["lnT_evV"] = lnst["evV"]
                    cb["ln3sp"] = lnst["sp"]
                    cb["lnG2"] = lnT_cT[3]
                    cb["lnN_guard"][0] = lnT_cT[2]
                    cb["lnN_guard"][1] = lnT_cT[3]
                    flush_all()
                    tap("x2_%d_%d" % (s_i, b_i),
                        lambda par_b=par_b: xblk[:, 4 * par_b:4 * par_b + 4, :]
                        .rearrange("p a b -> p (a b)"),
                        [128, 4 * 1024])
                V.wait(T, cb["lnG2"])
                V.wait(T, cb["attnT"])
                V.wait(S, cb["qk_ev"][0])
                V.wait(S, cb["qk_ev"][1])
                V.wait(S, cb["v_ev"][0])
                V.wait(S, cb["v_ev"][1])
                V.emit(lambda v: v.tensor_copy(
                    out=ktail[:], in_=kbuf[:, :, 512:640]))
                V.emit(lambda v: v.tensor_copy(out=vtail[:], in_=vbuf[:, 4, :]),
                       inc=True)
                flush_all()
                bar()
            if stop_after == "attn":
                continue
            with contextlib.ExitStack() as fst:
                e8_s = sb("e8_s", [8, 8, 128], BF16, fst)
                b2b = sb("b2b", [8, 1024], BF16, fst)
                QS.dma(lambda q: q.dma_start(
                    out=e8_s[:].rearrange("p a b -> p (a b)"),
                    in_=e8d[:, :, :].rearrange("p a b -> p (a b)")), smConstA)
                c_e8 = smConstA.n
                c_b2b = QG.dma(lambda g: g.dma_start(out=b2b[:, :],
                                                     in_=b2[:, :]), smB2b)
                T.wait(smConstA, c_e8)
                V.wait(smConstA, c_e8)
                S.wait(smConstA, c_e8)
                T.wait(smB2b, c_b2b)
                V.wait(smB2b, c_b2b)
                S.wait(smB2b, c_b2b)
                ln3f = sb("ln3f", [128, KCH, TSLICE], BF16, fst)
                y_tm = sb("y_tm", [128, 16, 1024], F32, fst)
                h2 = sb("h2", [128, 8, TSLICE], BF16, fst)
                gdf = sb("gdf", [8, TSLICE], BF16, fst)
                fstat = sb("fstat", [128, 16], F32, fst)
                fwst = fst.enter_context(contextlib.ExitStack())
                hmul = sb("hmul", [128, TSLICE], BF16, fwst)
                sa_t = sb("sa_t", [128, TSLICE], BF16, fwst)
                g_sb = sb("g_sb", [128, TSLICE], BF16, fwst)
                wst1 = sb("wst1", [128, 2, 2, 8, 128], BF16, fwst)
                wst3 = sb("wst3", [128, 2, 2, 8, 128], BF16, fwst)
                w2t_sb = sb("w2t_sb", [128, 8, 1024], BF16, fwst)
                glog = sb("glog", [8, 512], F32, fwst)
                gdt = sb("gdt", [128, 8], F32, fwst)
                gsc1 = sb("gsc1", [128, 8], F32, fwst)
                gdtb = sb("gdtb", [128, 8], BF16, fwst)

                cq_ln3 = QS.dma(lambda q, s=s_i: q.dma_start(
                    out=ln3f[:].rearrange("p a b -> p (a b)"),
                    in_=ln3_d[s].rearrange("p a b -> p (a b)")), smLn3)

                V.wait(smLn3, cq_ln3)
                T.wait(smLn3, cq_ln3)
                gl_ev = 0
                gdT_ev = 0
                glogT_used = 0
                for tb in range(4):
                    T.wait(S, gl_ev)
                    for k in range(KCH):
                        T.emit(lambda t, k=k, tb=tb: t.matmul(
                            ps[0:8, 6 * 512:6 * 512 + 512], gate_wT[:, k, :],
                            ln3f[:, k, tb * 512:(tb + 1) * 512],
                            start=(k == 0), stop=(k == KCH - 1)))
                    cT = T.milestone()
                    S.wait(T, cT)
                    S.wait(T, glogT_used)
                    gl_ev = S.emit(lambda s: s.activation(
                        out=glog[0:8, :], in_=ps[0:8, 6 * 512:6 * 512 + 512],
                        func=AF.Identity, bias=gateb_c[0:8, 0:1]), inc=True)
                    for tci in range(4):
                        tc = tb * 4 + tci
                        T.wait(S, gl_ev)
                        T.wait(V, gdT_ev)
                        T.emit(lambda t, tci=tci: t.matmul(
                            ps[:, 7 * 512:7 * 512 + 8],
                            glog[0:8, tci * 128:(tci + 1) * 128],
                            identf_s[0:8, 0:8], start=True, stop=True))
                        cT = T.milestone()
                        glogT_used = cT
                        V.wait(T, cT)
                        c0 = V.emit(lambda v: v.reduce_max(
                            out=fstat[:, 0:1], in_=ps[:, 7 * 512:7 * 512 + 8],
                            axis=AX.X, negate=True), inc=True)
                        cV = V.emit(lambda v: v.tensor_copy(
                            out=gdt[:, :], in_=ps[:, 7 * 512:7 * 512 + 8]),
                            inc=True)
                        gdT_ev = cV
                        S.wait(V, cV)
                        cS = S.emit(lambda s: s.activation(
                            out=gsc1[:, :], in_=gdt[:, :], func=AF.Exp,
                            bias=fstat[:, 0:1],
                            accum_out=fstat[:, 1:2]), inc=True)
                        V.wait(S, cS)
                        c0 = V.emit(lambda v: v.reciprocal(out=fstat[:, 2:3],
                                                           in_=fstat[:, 1:2]),
                                    inc=True)
                        V.wait(V, c0)
                        c0 = V.emit(lambda v: v.tensor_scalar(
                            out=gsc1[:, :], in0=gsc1[:, :],
                            scalar1=fstat[:, 2:3], scalar2=None, op0=ALU.mult),
                            inc=True)
                        V.wait(V, c0)
                        c0 = V.emit(lambda v: v.reduce_max(
                            out=fstat[:, 3:4], in_=gsc1[:, :], axis=AX.X),
                            inc=True)
                        V.wait(V, c0)
                        c0 = V.emit(lambda v: v.tensor_scalar(
                            out=gdt[:, :], in0=gsc1[:, :],
                            scalar1=fstat[:, 3:4], scalar2=None,
                            op0=ALU.is_ge), inc=True)
                        V.wait(V, c0)
                        c0 = V.emit(lambda v: v.scalar_tensor_tensor(
                            out=gdt[:, :], in0=gdt[:, :], scalar=-1e30,
                            in1=gsc1[:, :], op0=ALU.mult, op1=ALU.add),
                            inc=True)
                        V.wait(V, c0)
                        c0 = V.emit(lambda v: v.reduce_max(
                            out=fstat[:, 4:5], in_=gdt[:, :], axis=AX.X),
                            inc=True)
                        V.wait(V, c0)
                        c0 = V.emit(lambda v: v.tensor_scalar(
                            out=gdt[:, :], in0=gsc1[:, :],
                            scalar1=fstat[:, 4:5], scalar2=None,
                            op0=ALU.is_ge), inc=True)
                        V.wait(V, c0)
                        c0 = V.emit(lambda v: v.tensor_tensor(
                            out=gdt[:, :], in0=gdt[:, :], in1=gsc1[:, :],
                            op=ALU.mult), inc=True)
                        V.wait(V, c0)
                        cV = V.emit(lambda v: v.tensor_copy(out=gdtb[:, :],
                                                            in_=gdt[:, :]),
                                    inc=True)
                        T.wait(V, cV)
                        T.emit(lambda t: t.matmul(
                            ps[0:8, 7 * 512 + 128:7 * 512 + 256],
                            gdtb[:, :], identb[:], start=True, stop=True))
                        cT = T.milestone()
                        V.wait(T, cT)
                        cV = V.emit(lambda v, tc=tc: v.tensor_copy(
                            out=gdf[0:8, tc * 128:(tc + 1) * 128],
                            in_=ps[0:8, 7 * 512 + 128:7 * 512 + 256]),
                            inc=True)
                        gdT_ev = cV
                flush_all()
                bar()
                tap("gdf_%d" % s_i, lambda: gdf[0:8, :], [8, TSLICE], BF16)

                fjobs = [("sh", 0), ("sh", 1)] + [("ex", e) for e in range(EXP)]
                wld = 0
                prev_aev = 0
                prev_vev = 0
                prev_yev = 0
                prev_yT = 0
                prev_wldu = [0, 0]
                for jkind, jidx in fjobs:
                    for grp in range(4):
                        QS.wait(T, prev_wldu[grp % 2])
                        if jkind == "sh":
                            mbase = jidx * 8 + grp * 2
                            src1 = ws1T[:, mbase:mbase + 2, :, :]
                            src3 = ws3T[:, mbase:mbase + 2, :, :]
                        else:
                            mbase = grp * 2
                            src1 = w1T[jidx, :, mbase:mbase + 2, :, :]
                            src3 = w3T[jidx, :, mbase:mbase + 2, :, :]
                        QS.dma(lambda q, s1=src1, g=grp: q.dma_start(
                            out=wst1[:, g % 2].rearrange("p a b c -> p (a b c)"),
                            in_=s1.rearrange("p a b c -> p (a b c)")),
                            smFw[grp % 2])
                        wld = QS.dma(lambda q, s3=src3, g=grp: q.dma_start(
                            out=wst3[:, g % 2].rearrange("p a b c -> p (a b c)"),
                            in_=s3.rearrange("p a b c -> p (a b c)")),
                            smFw[grp % 2])
                        if grp == 0:
                            wld0 = wld
                        if grp == 1:
                            QS.wait(T, prev_yT)
                            if jkind == "sh":
                                c_w2t = QS.dma(lambda q, hf=jidx: q.dma_start(
                                    out=w2t_sb[:].rearrange("p a b -> p (a b)"),
                                    in_=ws2T[:, hf * 8:(hf + 1) * 8, :]
                                    .rearrange("p a b -> p (a b)")), smW2t)
                            else:
                                c_w2t = QS.dma(lambda q, e=jidx: q.dma_start(
                                    out=w2t_sb[:].rearrange("p a b -> p (a b)"),
                                    in_=w2T[e].rearrange("p a b -> p (a b)")),
                                    smW2t)
                        if grp == 1:
                            break
                    if jkind == "ex":
                        T.wait(S, prev_aev)
                        for tb in range(4):
                            T.emit(lambda t, e=jidx, tb=tb: t.matmul(
                                pb(4 + tb), e8_s[:, e, :],
                                gdf[0:8, tb * 512:(tb + 1) * 512],
                                start=True, stop=True))
                        cT = T.milestone()
                        S.wait(T, cT)
                        S.wait(V, prev_vev)
                        cS_g = S.emit(lambda s: s.copy(out=g_sb[:],
                                                       in_=pb(4, 0, 2048)),
                                      inc=True)
                    else:
                        cS_g = 0
                    flush_all()

                    T.wait(V, prev_yev)
                    T.wait(S, prev_aev)
                    bev = 0
                    aev = prev_aev
                    vev = prev_vev
                    wld_grp = [wld0, wld, 0, 0]
                    wld_used = [0, 0]
                    for mi in range(8):
                        gpar = (mi // 2) % 2
                        if mi >= 2 and mi % 2 == 0:
                            grp = mi // 2 + 1
                            if grp < 4:
                                if jkind == "sh":
                                    mb = jidx * 8 + grp * 2
                                    s1 = ws1T[:, mb:mb + 2, :, :]
                                    s3 = ws3T[:, mb:mb + 2, :, :]
                                else:
                                    mb = grp * 2
                                    s1 = w1T[jidx, :, mb:mb + 2, :, :]
                                    s3 = w3T[jidx, :, mb:mb + 2, :, :]
                                QS.wait(T, wld_used[grp % 2])
                                QS.dma(lambda q, s1=s1, g=grp: q.dma_start(
                                    out=wst1[:, g % 2].rearrange(
                                        "p a b c -> p (a b c)"),
                                    in_=s1.rearrange("p a b c -> p (a b c)")),
                                    smFw[grp % 2])
                                wld_grp[grp] = QS.dma(lambda q, s3=s3, g=grp:
                                              q.dma_start(
                                    out=wst3[:, g % 2].rearrange(
                                        "p a b c -> p (a b c)"),
                                    in_=s3.rearrange("p a b c -> p (a b c)")),
                                    smFw[grp % 2])
                        if mi % 2 == 0:
                            T.wait(smFw[(mi // 2) % 2], wld_grp[mi // 2])
                        T.wait(S, bev)
                        for k in range(KCH):
                            for tb in range(4):
                                T.emit(lambda t, mi=mi, k=k, tb=tb, gpar=gpar:
                                       t.matmul(
                                    pb(tb, 0, 512), wst3[:, gpar, mi % 2, k, :],
                                    ln3f[:, k, tb * 512:(tb + 1) * 512],
                                    start=(k == 0), stop=(k == KCH - 1)))
                        cT3 = T.milestone()
                        T.wait(S, aev)
                        T.wait(S, cS_g)
                        for k in range(KCH):
                            for tb in range(4):
                                T.emit(lambda t, mi=mi, k=k, tb=tb, gpar=gpar:
                                       t.matmul(
                                    pb(4 + tb, 0, 512),
                                    wst1[:, gpar, mi % 2, k, :],
                                    ln3f[:, k, tb * 512:(tb + 1) * 512],
                                    start=(k == 0), stop=(k == KCH - 1)))
                        cT1 = T.milestone()
                        if mi % 2 == 1:
                            wld_used[(mi // 2) % 2] = cT1
                        if jkind == "sh":
                            bb3 = bs3_c[:, jidx * 8 + mi:jidx * 8 + mi + 1]
                            bb1 = bs1_c[:, jidx * 8 + mi:jidx * 8 + mi + 1]
                        else:
                            bb3 = b3_c[:, jidx * 8 + mi:jidx * 8 + mi + 1]
                            bb1 = b1_c[:, jidx * 8 + mi:jidx * 8 + mi + 1]
                        S.wait(T, cT3)
                        bev = S.emit(lambda s, mi=mi, bb3=bb3: s.activation(
                            out=h2[:, mi, :], in_=pb(0, 0, 2048),
                            func=AF.Identity, bias=bb3), inc=True)
                        S.wait(T, cT1)
                        S.wait(V, vev)
                        aev = S.emit(lambda s, bb1=bb1: s.activation(
                            out=sa_t[:], in_=pb(4, 0, 2048), func=AF.Silu,
                            bias=bb1), inc=True)
                        V.wait(S, bev)
                        V.wait(V, vev)
                        if jkind == "ex":
                            V.wait(S, cS_g)
                            c0 = V.emit(lambda v, mi=mi: v.tensor_tensor(
                                out=hmul[:], in0=h2[:, mi, :], in1=g_sb[:],
                                op=ALU.mult), inc=True)
                        else:
                            c0 = V.emit(lambda v, mi=mi: v.tensor_copy(
                                out=hmul[:], in_=h2[:, mi, :]), inc=True)
                        V.wait(V, c0)
                        V.wait(S, aev)
                        vev = V.emit(lambda v, mi=mi: v.tensor_tensor(
                            out=h2[:, mi, :], in0=hmul[:], in1=sa_t[:],
                            op=ALU.mult), inc=True)
                    flush_all()

                    T.wait(S, bev)
                    T.wait(V, vev)
                    T.wait(smW2t, c_w2t)
                    yev = [0, 0]
                    for ts in range(16):
                        par = ts % 2
                        T.wait(V, yev[par])
                        for k in range(KCH):
                            for half in range(2):
                                T.emit(lambda t, ts=ts, k=k, half=half,
                                       par=par: t.matmul(
                                    pb(2 * par + half, 0, 512),
                                    h2[:, k, ts * 128:(ts + 1) * 128],
                                    w2t_sb[:, k, half * 512:(half + 1) * 512],
                                    start=(k == 0), stop=False
                                    if k < KCH - 1 else
                                    (jkind != "ex" or jidx != 0)))
                        if jkind == "ex" and jidx == 0:
                            for half in range(2):
                                T.emit(lambda t, ts=ts, half=half, par=par:
                                       t.matmul(
                                    pb(2 * par + half, 0, 512),
                                    gdf[0:8, ts * 128:(ts + 1) * 128],
                                    b2b[0:8, half * 512:(half + 1) * 512],
                                    start=False, stop=True))
                        cT = T.milestone()
                        V.wait(T, cT)
                        if jkind == "sh" and jidx == 0:
                            cV = V.emit(lambda v, ts=ts, par=par:
                                        v.tensor_tensor(
                                out=y_tm[:, ts, :], in0=pb(2 * par, 0, 1024),
                                in1=bs2_bc[:].rearrange("p a b -> p (a b)"),
                                op=ALU.add), inc=True)
                        else:
                            cV = V.emit(lambda v, ts=ts, par=par:
                                        v.tensor_tensor(
                                out=y_tm[:, ts, :], in0=pb(2 * par, 0, 1024),
                                in1=y_tm[:, ts, :], op=ALU.add), inc=True)
                        yev[par] = cV
                    prev_aev = aev
                    prev_vev = vev
                    prev_yev = max(yev[0], yev[1])
                    prev_yT = T.n
                    prev_wldu = [wld_used[0], wld_used[1]]
                    flush_all()
                tap("y_%d" % s_i,
                    lambda: y_tm[:].rearrange("p a b -> p (a b)"),
                    [128, 16 * 1024])
                bar()
                fwst.close()
                x2l = sb("x2l", [128, 2, 1024], F32, fst)
                jk2 = sb("jk2", [128, 1024], F32, fst)
                jk2v = sb("jk2v", [128, 1024], F32, fst)
                fstats = StatCtl(fstat, jk2v[:])

                x2st = [0, 0]
                st_done = 0
                prev_add = 0
                for ts in range(16):
                    par = ts % 2
                    QS.wait(smOut[par], x2st[par])
                    cq = QS.dma(lambda q, s=s_i, ts=ts, par=par: q.dma_start(
                        out=x2l[:, par, :], in_=x2_d[s, ts]), smX2l[par])
                    S.wait(V, prev_add)
                    si_, r_, nb_ = fstats.compute(
                        lambda ts=ts: y_tm[:, ts, :], jk2[:])
                    cs = S.emit(lambda s, ts=ts, r_=r_, nb_=nb_: s.activation(
                        out=jk2[:], in_=y_tm[:, ts, :], func=AF.Identity,
                        bias=nb_, scale=r_), inc=True)
                    fstats.consumed(si_, cs)
                    V.wait(S, cs)
                    V.wait(smX2l[par], cq)
                    cV = V.emit(lambda v, par=par: v.tensor_tensor(
                        out=x2l[:, par, :], in0=x2l[:, par, :], in1=jk2[:],
                        op=ALU.add), inc=True)
                    prev_add = cV
                    QS.wait(V, cV)
                    st_done = QS.dma(lambda q, s=s_i, ts=ts, par=par:
                                     q.dma_start(
                        out=y_out[(s * 16 + ts) * 128:(s * 16 + ts + 1) * 128,
                                  :], in_=x2l[:, par, :]), smOut[par])
                    x2st[par] = st_done
                flush_all()
                bar()
        bar()
        flush_all()
    return nc, dbg_outs


def host_consts(h):
    bf = ml_dtypes.bfloat16
    ident = np.eye(128, dtype=np.float32)
    qi = np.arange(128)[None, :]
    kj = np.arange(128)[:, None]
    mA = (kj >= qi).astype(bf)
    mB = (kj < qi).astype(bf)
    mask_std = np.concatenate([mA, mB], axis=1)
    if h == 0:
        mask_first = np.concatenate([np.zeros_like(mA), mB], axis=1)
    else:
        mask_first = mask_std
    maskT = np.stack([mask_std, mask_first], axis=0)
    lat = np.arange(NLAT)[:, None]
    gchunk = 32 * h + np.arange(32)[None, :]
    ml = np.where(lat < gchunk, 0.0, -1e9).astype(np.float32)
    hh = np.arange(16)[:, None, None]
    cc = np.arange(8)[None, :, None]
    mm = np.arange(128)[None, None, :]
    ehmat = (hh == 2 * cc + mm // 64).astype(bf)
    kk = np.arange(8)[:, None, None]
    ee = np.arange(8)[None, :, None]
    e8 = np.broadcast_to(kk == ee, (8, 8, 128)).astype(bf)
    return {"identf": ident, "maskTd": maskT, "mld": ml,
            "ehmatd": np.ascontiguousarray(ehmat),
            "e8d": np.ascontiguousarray(e8)}


def make_in_maps(inputs):
    f32 = np.float32
    x = np.asarray(inputs["L0_attn_x"], dtype=f32)
    wmap = {
        "qkv_w": "L0_attn_qkv_w", "qkv_b": "L0_attn_qkv_b",
        "rq_w": "L0_attn_rq_w", "rq_b": "L0_attn_rq_b",
        "rkv_w": "L0_attn_rkv_w", "rkv_b": "L0_attn_rkv_b",
        "rout_w": "L0_attn_rout_w", "rout_b": "L0_attn_rout_b",
        "out_w": "L0_attn_out_w", "out_b": "L0_attn_out_b",
        "gl_w": "L0_attn_gl_w", "gr_w": "L0_attn_gr_w",
        "gate_w": "L0_ffn_gate_w", "gate_b": "L0_ffn_gate_b",
        "w1": "L0_ffn_w1", "b1": "L0_ffn_b1", "w3": "L0_ffn_w3",
        "b3": "L0_ffn_b3", "w2": "L0_ffn_w2", "b2": "L0_ffn_b2",
        "ws1": "L0_ffn_ws1", "bs1": "L0_ffn_bs1",
        "ws3": "L0_ffn_ws3", "bs3": "L0_ffn_bs3",
        "ws2": "L0_ffn_ws2", "bs2": "L0_ffn_bs2",
    }
    ws = {k: np.ascontiguousarray(np.asarray(inputs[v], dtype=f32))
          for k, v in wmap.items()}
    consts = [host_consts(0), host_consts(1)]
    zeros_halo = np.zeros((128, D), dtype=f32)
    in_maps = []
    for c in range(8):
        b, h = c // 2, c % 2
        m = dict(ws)
        m.update(consts[h])
        m["x_own"] = np.ascontiguousarray(x[b, h * TCORE:(h + 1) * TCORE])
        m["x_full"] = np.ascontiguousarray(x[b])
        m["x_halo"] = (zeros_halo if h == 0 else
                       np.ascontiguousarray(x[b, TCORE - 128:TCORE]))
        in_maps.append(m)
    return in_maps


_BUILD_CACHE = {}


def get_nc(dbg_key=None):
    key = dbg_key or tuple(sorted(DEBUG.items()))
    if key not in _BUILD_CACHE:
        _BUILD_CACHE[key] = build_nc(DEBUG)
    return _BUILD_CACHE[key]


def kernel(**inputs):
    nc, _ = get_nc()
    in_maps = make_in_maps(inputs)
    res = run_bass_kernel_spmd(nc, in_maps, core_ids=list(range(8)))
    out = np.empty((4, S_FULL, D), dtype=np.float32)
    for c in range(8):
        b, h = c // 2, c % 2
        out[b, h * TCORE:(h + 1) * TCORE] = res.results[c]["out"]
    return out


# version 11
# speedup vs baseline: 1.0396x; 1.0396x over previous
import contextlib
import numpy as np
import ml_dtypes
import concourse.bass as bass
import concourse.mybir as mybir
from concourse.bass_utils import run_bass_kernel_spmd

F32 = mybir.dt.float32
BF16 = mybir.dt.bfloat16
AF = mybir.ActivationFunctionType
ALU = mybir.AluOpType
AX = mybir.AxisListType

D = 1024
KCH = 8
S_FULL = 8192
TCORE = 4096
TSLICE = 2048
NSLICE = 2
NBLK = 4
BT = 512
NSUB = 4
H = 16
HDIM = 64
LH = 4
LD = 256
NLAT = 64
EXP = 8
EPS = 1e-5

DEBUG = {}


class DSem:

    def __init__(self, handle):
        self.sem = handle
        self.n = 0


class Rec:
    def __init__(self, sem, step=1):
        self.sem = sem
        self.step = step
        self.n = 0
        self.ops = []

    def dma(self, fn, ds):
        ds.n += 16
        self.ops.append((lambda e, f=fn, d=ds: f(e).then_inc(d.sem, 16), -2))
        return ds.n

    def emit(self, fn, inc=False):
        if inc:
            self.n += self.step
            self.ops.append((fn, self.step))
            return self.n
        self.ops.append((fn, 0))
        return self.n

    def milestone(self):
        fn, inc = self.ops[-1]
        assert inc == 0, "last op already incs/waits"
        self.n += self.step
        self.ops[-1] = (fn, self.step)
        return self.n

    def wait(self, rec, val):
        if val <= 0:
            return
        self.ops.append((lambda e, s=rec.sem, v=val: e.wait_ge(s, v), -1))

    def wait_all(self, recs_or_dsems):
        for r in recs_or_dsems:
            self.wait(r, r.n)

    def flush(self, section):
        ops = self.ops
        self.ops = []
        if not ops:
            return

        def body(e):
            for fn, inc in ops:
                ins = fn(e)
                if inc > 0:
                    ins.then_inc(self.sem, inc)
        section(body)


def build_nc(dbgcfg=None, stop_after=None):
    dbgcfg = dbgcfg or {}
    nc = bass.Bass("TRN2", target_bir_lowering=False)

    def din(name, shape, dt=F32):
        return nc.dram_tensor(name, list(shape), dt, kind="ExternalInput").ap()

    x_own = din("x_own", [TCORE, D])
    x_full = din("x_full", [S_FULL, D])
    x_halo = din("x_halo", [128, D])
    qkv_w = din("qkv_w", [D, 3 * D]); qkv_b = din("qkv_b", [3 * D])
    rq_w = din("rq_w", [D, LD]); rq_b = din("rq_b", [LD])
    rkv_w = din("rkv_w", [D, 2 * LD]); rkv_b = din("rkv_b", [2 * LD])
    rout_w = din("rout_w", [LD, D]); rout_b = din("rout_b", [D])
    out_w = din("out_w", [D, D]); out_b = din("out_b", [D])
    gl_w = din("gl_w", [HDIM, 1]); gr_w = din("gr_w", [HDIM, 1])
    gate_w = din("gate_w", [EXP, D]); gate_b = din("gate_b", [EXP])
    w1 = din("w1", [EXP, D, D]); b1 = din("b1", [EXP, D])
    w3 = din("w3", [EXP, D, D]); b3 = din("b3", [EXP, D])
    w2 = din("w2", [EXP, D, D]); b2 = din("b2", [EXP, D])
    ws1 = din("ws1", [2 * D, D]); bs1 = din("bs1", [2 * D])
    ws3 = din("ws3", [2 * D, D]); bs3 = din("bs3", [2 * D])
    ws2 = din("ws2", [D, 2 * D]); bs2 = din("bs2", [D])
    identf = din("identf", [128, 128])
    maskTd = din("maskTd", [2, 128, 256], BF16)
    mld = din("mld", [NLAT, 32])
    ehmatd = din("ehmatd", [16, 8, 128], BF16)
    e8d = din("e8d", [8, 8, 128], BF16)

    y_out = nc.dram_tensor("out", [TCORE, D], F32, kind="ExternalOutput").ap()

    def scr(name, shape, dt=BF16):
        return nc.dram_tensor(name, list(shape), dt, kind="Internal").ap()

    qkv_wb = scr("qkv_wb", [D, 3 * D])
    rq_wb = scr("rq_wb", [D, LD])
    rkv_wb = scr("rkv_wb", [D, 2 * LD])
    rout_wb = scr("rout_wb", [LD, D])
    out_wb = scr("out_wb", [D, D])
    w1T = scr("w1T", [EXP, 128, 8, 8, 128])
    w3T = scr("w3T", [EXP, 128, 8, 8, 128])
    ws1T = scr("ws1T", [128, 16, 8, 128])
    ws3T = scr("ws3T", [128, 16, 8, 128])
    w2T = scr("w2T", [EXP, 128, 8, 1024])
    ws2T = scr("ws2T", [128, 16, 1024])
    ln3_d = scr("ln3_d", [NSLICE, 128, KCH, TSLICE])
    x2_d = nc.dram_tensor("x2_d", [NSLICE, 16, 128, D], F32,
                          kind="Internal").ap()

    dbg_outs = {}

    st = contextlib.ExitStack()
    with st:
        sem_dS = st.enter_context(nc.semaphore("dS"))
        sem_dG = st.enter_context(nc.semaphore("dG"))
        sem_T = st.enter_context(nc.semaphore("semT"))
        sem_V = st.enter_context(nc.semaphore("semV"))
        sem_S = st.enter_context(nc.semaphore("semS"))
        block = st.enter_context(nc.Block())

        QS = Rec(sem_dS, 16)
        QG = Rec(sem_dG, 16)
        T = Rec(sem_T)
        V = Rec(sem_V)
        S = Rec(sem_S)
        _dsems = []

        def mk_dsem(name):
            d = DSem(st.enter_context(nc.semaphore(name)))
            _dsems.append(d)
            return d

        smCast = mk_dsem("smCast")
        smConstA = mk_dsem("smConstA")
        smB2b = mk_dsem("smB2b")
        smBrow = mk_dsem("smBrow")
        smGw = mk_dsem("smGw")
        smRkvw = mk_dsem("smRkvw")
        smWst = [mk_dsem("smWst0"), mk_dsem("smWst1")]
        smTst = [mk_dsem("smTst0"), mk_dsem("smTst1")]
        smW2st = mk_dsem("smW2st")
        smXs = [mk_dsem("smXs0"), mk_dsem("smXs1")]
        smProw2 = [mk_dsem("smProwA"), mk_dsem("smProwB")]
        smObr = mk_dsem("smObr")
        smWsl = mk_dsem("smWsl")
        smGl = mk_dsem("smGl")
        smGlc = mk_dsem("smGlc")
        smX4 = [mk_dsem("smX4_%d" % i) for i in range(4)]
        smSpill = mk_dsem("smSpill")
        smSpillL = mk_dsem("smSpillL")
        smLn3 = mk_dsem("smLn3")
        smFw = [mk_dsem("smFw0"), mk_dsem("smFw1")]
        smW2t = mk_dsem("smW2t")
        smX2l = [mk_dsem("smX2l0"), mk_dsem("smX2l1")]
        smOut = [mk_dsem("smOut0"), mk_dsem("smOut1")]
        smHalo = mk_dsem("smHalo")
        smTap = mk_dsem("smTap")
        ALL = [T, V, S]

        def flush_all():
            QS.flush(block.sync)
            QG.flush(block.gpsimd)
            T.flush(block.tensor)
            V.flush(block.vector)
            S.flush(block.scalar)

        def bar():
            snaps = [(r, r.n) for r in ALL]
            dsnaps = [(d, d.n) for d in _dsems]
            for r in [T, V, S, QS, QG]:
                for r2, v in snaps:
                    if r2 is not r:
                        r.wait(r2, v)
                for d, v in dsnaps:
                    r.wait(d, v)

        _sbctr = [0]

        def sb(name, shape, dt, stack):
            _sbctr[0] += 1
            return stack.enter_context(
                nc.sbuf_tensor(f"{name}_{_sbctr[0]}", list(shape), dt))

        def tap(name, src_fn, shape, dt=F32):
            if not dbgcfg.get(name):
                return
            if name not in dbg_outs:
                dbg_outs[name] = nc.dram_tensor(
                    "dbg_" + name, list(shape), dt, kind="ExternalOutput").ap()
            d = dbg_outs[name]
            bar()
            QG.dma(lambda g: g.dma_start(out=d, in_=src_fn()), smTap)
            bar()

        identb = sb("identb", [128, 128], BF16, st)
        identf_s = sb("identf_s", [128, 128], F32, st)
        ones_b = sb("ones_b", [128, 1], BF16, st)
        onescol_f = sb("onescol_f", [128, 1], F32, st)
        onesrow_f = sb("onesrow_f", [1, 128], F32, st)
        qkvb_c = sb("qkvb_c", [128, 24], F32, st)
        qkvbq_c = sb("qkvbq_c", [128, 8], F32, st)
        rqbq_c = sb("rqbq_c", [128, 2], F32, st)
        rkvb_c = sb("rkvb_c", [128, 4], F32, st)
        routb_c = sb("routb_c", [128, 8], F32, st)
        b1_c = sb("b1_c", [128, 64], F32, st)
        b3_c = sb("b3_c", [128, 64], F32, st)
        bs1_c = sb("bs1_c", [128, 16], F32, st)
        bs3_c = sb("bs3_c", [128, 16], F32, st)
        gateb_c = sb("gateb_c", [8, 1], F32, st)
        bs2_bc = sb("bs2_bc", [128, KCH, 128], F32, st)

        ml_s = sb("ml_s", [NLAT, 32], F32, st)
        gate_wT = sb("gate_wT", [128, 8, 8], BF16, st)

        rk_fm = sb("rk_fm", [128, 2, NLAT], BF16, st)
        rv_tm = sb("rv_tm", [NLAT, 2, 128], BF16, st)
        ktail = sb("ktail", [128, KCH, 128], BF16, st)
        epsc = sb("epsc", [128, 1], F32, st)
        vtail = sb("vtail", [128, 1024], BF16, st)
        obrow2 = sb("obrow2", [1, 2 * D], BF16, st)
        onesrow_b = sb("onesrow_b", [1, 128], BF16, st)

        ps = st.enter_context(nc.psum_tensor("ps", [128, 4096], F32))

        def dummy_mm(bank):
            T.emit(lambda t: t.matmul(
                ps[0:1, bank * 512:bank * 512 + 1], onesrow_b[0:1, 0:1],
                onesrow_b[0:1, 0:1], start=True, stop=True))

        def pb(b0, o=0, n=512):
            return ps[:, 512 * b0 + o: 512 * b0 + o + n]

        class StatCtl:
            def __init__(self, tile, vjk):
                self.tile = tile
                self.vjk = vjk
                self.i = 0
                self.ns = tile.shape[1] // 5
                self.s_done = [0] * self.ns
                self.v_done = [0] * self.ns

            def compute(self, src_fn, jk):
                i = self.i
                self.i = (self.i + 1) % self.ns
                t = self.tile
                b0 = i * 5
                sm = t[:, b0:b0 + 1]
                sq = t[:, b0 + 1:b0 + 2]
                r_ = t[:, b0 + 2:b0 + 3]
                nb = t[:, b0 + 3:b0 + 4]
                tmp = t[:, b0 + 4:b0 + 5]
                V.wait(S, self.s_done[i])
                csm = V.emit(lambda v: v.scalar_tensor_tensor(
                    out=jk, in0=src_fn(), scalar=1.0, in1=src_fn(),
                    op0=ALU.mult, op1=ALU.bypass, accum_out=sm), inc=True)
                c = V.emit(lambda v: v.scalar_tensor_tensor(
                    out=self.vjk, in0=src_fn(), scalar=1.0, in1=src_fn(),
                    op0=ALU.mult, op1=ALU.mult, accum_out=sq), inc=True)
                V.wait(V, c)
                c = V.emit(lambda v: v.tensor_scalar(out=sq, in0=sq,
                                                     scalar1=1.0 / D,
                                                     scalar2=None,
                                                     op0=ALU.mult), inc=True)
                V.wait(V, csm)
                c2m = V.emit(lambda v: v.tensor_scalar(out=sm, in0=sm,
                                                       scalar1=1.0 / D,
                                                       scalar2=None,
                                                       op0=ALU.mult), inc=True)
                V.wait(V, c2m)
                ctm = V.emit(lambda v: v.tensor_tensor(out=tmp, in0=sm, in1=sm,
                                                       op=ALU.mult), inc=True)
                V.wait(V, max(c, ctm))
                c = V.emit(lambda v: v.tensor_tensor(out=sq, in0=sq, in1=tmp,
                                                     op=ALU.subtract), inc=True)
                S.wait(V, c)
                cln = S.emit(lambda s: s.activation(out=tmp, in_=sq, func=AF.Ln,
                                                    bias=epsc[:], scale=1.0),
                             inc=True)
                S.wait(S, cln)
                c2 = S.emit(lambda s: s.activation(out=r_, in_=tmp, func=AF.Exp,
                                                   bias=0.0, scale=-0.5),
                            inc=True)
                V.wait(S, c2)
                cv = V.emit(lambda v: v.tensor_scalar(out=nb, in0=sm,
                                                      scalar1=r_, scalar2=-1.0,
                                                      op0=ALU.mult,
                                                      op1=ALU.mult), inc=True)
                self.v_done[i] = cv
                S.wait(S, c2)
                S.wait(V, cv)
                return i, r_, nb

            def consumed(self, i, s_count):
                self.s_done[i] = s_count

        for (src_, dst_) in [(qkv_w, qkv_wb), (rq_w, rq_wb), (rkv_w, rkv_wb),
                             (rout_w, rout_wb), (out_w, out_wb)]:
            QG.dma(lambda g, s=src_, d=dst_: g.dma_start(out=d[:, :],
                                                         in_=s[:, :]), smCast)
        c_wcast = smCast.n
        QS.dma(lambda q: q.dma_start(out=identf_s[:], in_=identf[:, :]),
               smConstA)
        QS.dma(lambda q: q.dma_start(out=ml_s[:], in_=mld[:, :]), smConstA)
        c_consts = smConstA.n
        V.wait(smConstA, c_consts)
        V.emit(lambda v: v.memset(ones_b[:], 1.0))
        V.emit(lambda v: v.memset(epsc[:], EPS))
        V.emit(lambda v: v.memset(onescol_f[:], 1.0))
        V.emit(lambda v: v.memset(onesrow_f[:], 1.0))
        V.emit(lambda v: v.memset(onesrow_b[:], 1.0))
        V.emit(lambda v: v.tensor_copy(out=identb[:], in_=identf_s[:]), inc=True)
        bar()
        if stop_after == "cast":
            bar()
            flush_all()
            return nc, dbg_outs

        bias_specs = [
            ("qkv_b", qkv_b, 24), ("rq_b", rq_b, 2), ("rkv_b", rkv_b, 4),
            ("rout_b", rout_b, 8), ("bs1", bs1, 16), ("bs3", bs3, 16),
            ("b1", b1.rearrange("e i -> (e i)"), 64),
            ("b3", b3.rearrange("e i -> (e i)"), 64),
        ]
        boff = {}
        off = 0
        for nm, _, n in bias_specs:
            boff[nm] = off
            off += n
        nbch = off

        with contextlib.ExitStack() as past:
            brow = sb("brow", [1, nbch * 128 + 128], F32, past)
            obrow = sb("obrow", [1, 2 * D], F32, past)
            QS.dma(lambda q: q.dma_start(out=obrow[0:1, 0:D],
                                         in_=out_b[None, :]), smObr)
            c_ob = QS.dma(lambda q: q.dma_start(out=obrow[0:1, D:2 * D],
                                                in_=bs2[None, :]), smObr)
            V.wait(smObr, c_ob)
            cvo = V.emit(lambda v: v.tensor_copy(out=obrow2[0:1, 0:D],
                                                 in_=obrow[0:1, 0:D]),
                         inc=True)
            V.wait(V, cvo)
            V.emit(lambda v: v.tensor_tensor(out=obrow2[0:1, D:2 * D],
                                             in0=obrow[0:1, 0:D],
                                             in1=obrow2[0:1, 0:D],
                                             op=ALU.subtract), inc=True)
            gb_off = nbch * 128
            for nm, ap_, n in bias_specs:
                QS.dma(lambda q, a=ap_, o=boff[nm], m=n: q.dma_start(
                    out=brow[0:1, o * 128:(o + m) * 128], in_=a[None, :]),
                    smBrow)
            c_bld = QS.dma(lambda q: q.dma_start(
                out=brow[0:1, gb_off:gb_off + 8], in_=gate_b[None, :]), smBrow)
            T.wait(smBrow, c_bld)
            tgts = [("qkv_b", qkvb_c, 24), ("rkv_b", rkvb_c, 4),
                    ("rout_b", routb_c, 8), ("bs1", bs1_c, 16),
                    ("bs3", bs3_c, 16), ("b1", b1_c, 64), ("b3", b3_c, 64),
                    ("rq_b", rqbq_c, 2)]
            for nm, tgt, n in tgts:
                o = boff[nm]
                for j in range(n):
                    T.emit(lambda t, o=o, j=j: t.matmul(
                        ps[:, 6 * 512 + j:6 * 512 + j + 1],
                        brow[0:1, (o + j) * 128:(o + j + 1) * 128],
                        onesrow_f[0:1, 0:1], start=True, stop=True))
                cT = T.milestone()
                S.wait(T, cT)
                cS = S.emit(lambda s, tgt=tgt, n=n: s.copy(out=tgt[:, 0:n],
                                                           in_=pb(6, 0, n)),
                            inc=True)
                T.wait(S, cS)
            if stop_after == "p0a1":
                flush_all()
                bar()
                flush_all()
                return nc, dbg_outs
            T.emit(lambda t: t.matmul(ps[0:8, 6 * 512:6 * 512 + 1],
                                      brow[0:1, gb_off:gb_off + 8],
                                      onesrow_f[0:1, 0:1], start=True, stop=True))
            cT = T.milestone()
            S.wait(T, cT)
            S.emit(lambda s: s.copy(out=gateb_c[0:8, 0:1],
                                    in_=ps[0:8, 6 * 512:6 * 512 + 1]), inc=True)
            bar()
            bc_ev = [0, 0]
            for k in range(KCH):
                reg = pb(6 + k % 2, 0, 128)
                T.wait(S, bc_ev[k % 2])
                T.emit(lambda t, k=k, reg=reg: t.matmul(
                    reg, onesrow_f[0:1, :],
                    obrow[0:1, D + k * 128:D + (k + 1) * 128],
                    start=True, stop=True))
                cT = T.milestone()
                S.wait(T, cT)
                cS = S.emit(lambda s, k=k, reg=reg: s.copy(out=bs2_bc[:, k, :],
                                                           in_=reg), inc=True)
                bc_ev[k % 2] = cS
            if stop_after == "p0a2":
                flush_all()
                bar()
                flush_all()
                return nc, dbg_outs
            V.wait(S, S.n)
            V.emit(lambda v: v.tensor_scalar(out=qkvbq_c[:], in0=qkvb_c[:, 0:8],
                                             scalar1=0.125, scalar2=None,
                                             op0=ALU.mult))
            V.emit(lambda v: v.tensor_scalar(out=rqbq_c[:], in0=rqbq_c[:],
                                             scalar1=0.125, scalar2=None,
                                             op0=ALU.mult), inc=True)
            bar()
            gwrow = sb("gwrow", [8, 1024], BF16, past)
            cq = QG.dma(lambda g: g.dma_start(out=gwrow[:, :],
                                              in_=gate_w[:, :]), smGw)
            T.wait(smGw, cq)
            for k in range(KCH):
                reg = ps[:, 5 * 512 + (k % 4) * 8:5 * 512 + (k % 4) * 8 + 8]
                T.emit(lambda t, k=k, reg=reg: t.matmul(
                    reg, gwrow[0:8, k * 128:(k + 1) * 128], identb[0:8, 0:8],
                    start=True, stop=True))
                cT = T.milestone()
                S.wait(T, cT)
                cS = S.emit(lambda s, k=k, reg=reg: s.activation(
                    out=gate_wT[:, k, :], in_=reg, func=AF.Identity), inc=True)
                T.wait(S, cS)
            flush_all()
            bar()
            if stop_after == "p0a":
                flush_all()
                return nc, dbg_outs

        with contextlib.ExitStack() as pbst:
            NWB = 4
            wstage = [sb(f"wstage{i}", [128, 1024], BF16, pbst)
                      for i in range(NWB)]
            tstage = [sb(f"tstage{i}", [128, 8, 128], BF16, pbst)
                      for i in range(NWB)]
            w2stage = sb("w2stage", [128, 8, 1024], BF16, pbst)
            smWb = [mk_dsem("smWb%d" % i) for i in range(NWB)]
            smTb = [mk_dsem("smTb%d" % i) for i in range(NWB)]

            xs = [sb(f"xs{i}", [128, 4, 1024], F32, pbst) for i in range(2)]
            jk0 = sb("jk0", [128, 1024], F32, pbst)
            jk0v = sb("jk0v", [128, 1024], F32, pbst)
            ln1w = sb("ln1w", [128, 4, 1024], F32, pbst)
            pooled_tm = sb("pooled_tm", [NLAT, 1024], F32, pbst)
            prow = sb("prow", [1, 2048], F32, pbst)
            pstat_t = sb("pstat_t", [128, 32], F32, pbst)
            pooled_fmb = sb("pooled_fmb", [128, 8, NLAT], BF16, pbst)
            rkvw_sb = sb("rkvw_sb", [128, 8, 512], BF16, pbst)
            rvf = sb("rvf", [128, 2, NLAT], BF16, pbst)
            pstats = StatCtl(pstat_t, jk0v[:])

            QS.wait(smCast, c_wcast)
            c_rkvw = QS.dma(lambda q: q.dma_start(
                out=rkvw_sb[:],
                in_=rkv_wb.rearrange("(a p) b -> p a b", p=128)), smRkvw)

            xf = x_full.rearrange("(n p) d -> n p d", p=128)
            pc_state = {"xs_freed": [0, 0], "pool_ev": [0, 0],
                        "prow_dma": [0, 0], "ln1w_used": 0}

            def pool_stats(blkI):
                par = blkI % 2
                QS.wait(S, pc_state["xs_freed"][par])
                cq = QS.dma(lambda q, b=blkI, p=par: q.dma_start(
                    out=xs[p][:],
                    in_=xf[4 * b:4 * b + 4, :, :].rearrange("a p b -> p a b")),
                    smXs[par])
                V.wait(smXs[par], cq)
                S.wait(smXs[par], cq)
                applies = []
                for tc in range(4):
                    si_, r_, nb_ = pstats.compute(
                        lambda p=par, tc=tc: xs[p][:, tc, :], jk0[:])
                    S.wait(T, pc_state["ln1w_used"] if tc == 0 else 0)
                    cs = S.emit(lambda s, p=par, tc=tc, r_=r_, nb_=nb_:
                                s.activation(out=ln1w[:, tc, :],
                                             in_=xs[p][:, tc, :],
                                             func=AF.Identity, bias=nb_,
                                             scale=r_), inc=True)
                    pstats.consumed(si_, cs)
                    applies.append(cs)
                pc_state["xs_freed"][par] = applies[-1]
                return applies

            def pool_mms(blkI, applies):
                for tc in range(4):
                    chunkid = blkI * 4 + tc
                    ppar = chunkid % 2
                    T.wait(S, applies[tc])
                    T.wait(S, pc_state["pool_ev"][ppar])
                    b0 = 4 + 2 * ppar
                    T.emit(lambda t, b0=b0, tc=tc: t.matmul(
                        ps[0:1, b0 * 512:b0 * 512 + 512],
                        onescol_f[:], ln1w[:, tc, 0:512], start=True,
                        stop=True))
                    T.emit(lambda t, b0=b0, tc=tc: t.matmul(
                        ps[0:1, (b0 + 1) * 512:(b0 + 1) * 512 + 512],
                        onescol_f[:], ln1w[:, tc, 512:1024], start=True,
                        stop=True))
                    cT = T.milestone()
                    pc_state["ln1w_used"] = cT
                    S.wait(T, cT)
                    S.wait(smProw2[ppar], pc_state["prow_dma"][ppar])
                    cs2 = S.emit(lambda s, b0=b0, pp=ppar: s.activation(
                        out=prow[0:1, pp * 1024:pp * 1024 + 1024],
                        in_=ps[0:1, b0 * 512:b0 * 512 + 1024],
                        func=AF.Identity, scale=1.0 / 128.0), inc=True)
                    pc_state["pool_ev"][ppar] = cs2
                    QG.wait(S, cs2)
                    pc_state["prow_dma"][ppar] = QG.dma(
                        lambda g, c=chunkid, pp=ppar: g.dma_start(
                            out=pooled_tm[c:c + 1, :],
                            in_=prow[0:1, pp * 1024:pp * 1024 + 1024]),
                        smProw2[ppar])

            jobs = []
            for e in range(EXP):
                for rb in range(8):
                    jobs.append((w1[e, rb * 128:(rb + 1) * 128, :],
                                 w1T[e, :, rb, :, :]))
                    jobs.append((w3[e, rb * 128:(rb + 1) * 128, :],
                                 w3T[e, :, rb, :, :]))
            for rb in range(16):
                jobs.append((ws1[rb * 128:(rb + 1) * 128, :],
                             ws1T[:, rb, :, :]))
                jobs.append((ws3[rb * 128:(rb + 1) * 128, :],
                             ws3T[:, rb, :, :]))

            load_used = [0] * NWB
            psum_free = [0, 0]
            store_sn = [0] * NWB
            pool_next = [0]

            def prep_job(i, src_, dst_):
                par = i % NWB
                ppair = i % 2
                QG.wait(T, load_used[par])
                cq = QG.dma(lambda g, s=src_, p=par: g.dma_start(
                    out=wstage[p][:], in_=s), smWb[par])
                T.wait(smWb[par], cq)
                T.wait(S, psum_free[ppair])
                for k in range(KCH):
                    T.emit(lambda t, k=k, p=ppair, w=wstage[par]: t.matmul(
                        pb(2 * p + k // 4, (k % 4) * 128, 128),
                        w[:, k * 128:(k + 1) * 128], identb[:],
                        start=True, stop=True))
                cT = T.milestone()
                load_used[par] = cT
                S.wait(T, cT)
                S.wait(smTb[par], store_sn[par])
                cS = S.emit(lambda s, p=par: s.copy(
                    out=tstage[p][:].rearrange("p a b -> p (a b)"),
                    in_=pb(2 * ppair, 0, 1024)), inc=True)
                psum_free[ppair] = cS
                QS.wait(S, cS)
                store_sn[par] = QS.dma(lambda q, d=dst_, p=par: q.dma_start(
                    out=d.rearrange("p a b -> p (a b)"),
                    in_=tstage[p][:].rearrange("p a b -> p (a b)")), smTb[par])

            pend = [None, None]
            for i, (src_, dst_) in enumerate(jobs):
                prep_job(i, src_, dst_)
                if i % 10 == 9 and (pool_next[0] < 16 or pend[0] is not None):
                    if pend[0] is not None:
                        pool_mms(*pend[0])
                    if pool_next[0] < 16:
                        pend[0] = (pool_next[0], pool_stats(pool_next[0]))
                        pool_next[0] += 1
                    else:
                        pend[0] = None
            while pend[0] is not None or pool_next[0] < 16:
                if pend[0] is not None:
                    pool_mms(*pend[0])
                    pend[0] = None
                if pool_next[0] < 16:
                    pend[0] = (pool_next[0], pool_stats(pool_next[0]))
                    pool_next[0] += 1

            w2jobs = [([w2[e, rb * 128:(rb + 1) * 128, :] for rb in range(8)],
                       w2T[e]) for e in range(EXP)]
            w2jobs.append(([ws2[rb * 128:(rb + 1) * 128, 0:1024]
                            for rb in range(8)], ws2T[:, 0:8, :]))
            w2jobs.append(([ws2[rb * 128:(rb + 1) * 128, 1024:2048]
                            for rb in range(8)], ws2T[:, 8:16, :]))
            w2_store = 0
            for srcs, dst_ in w2jobs:
                cS = 0
                for rb in range(8):
                    par = rb % NWB
                    ppair = rb % 2
                    QG.wait(T, load_used[par])
                    cq = QG.dma(lambda g, s=srcs[rb], p=par: g.dma_start(
                        out=wstage[p][:], in_=s), smWb[par])
                    T.wait(smWb[par], cq)
                    T.wait(S, psum_free[ppair])
                    for k in range(KCH):
                        T.emit(lambda t, k=k, p=ppair, w=wstage[par]: t.matmul(
                            pb(2 * p + k // 4, (k % 4) * 128, 128),
                            w[:, k * 128:(k + 1) * 128],
                            identb[:], start=True, stop=True))
                    cT = T.milestone()
                    load_used[par] = cT
                    S.wait(T, cT)
                    if rb < 2:
                        S.wait(smW2st, w2_store)
                    cS = S.emit(lambda s, rb=rb, p=ppair: s.copy(
                        out=w2stage[:, :, rb * 128:(rb + 1) * 128],
                        in_=pb(2 * p, 0, 1024).rearrange("p (a b) -> p a b",
                                                         b=128)), inc=True)
                    psum_free[ppair] = cS
                QS.wait(S, cS)
                w2_store = QS.dma(lambda q, d=dst_: q.dma_start(
                    out=d.rearrange("p a b -> p (a b)"),
                    in_=w2stage[:].rearrange("p a b -> p (a b)")), smW2st)
            flush_all()
            bar()
            if stop_after == "p0b":
                flush_all()
                return nc, dbg_outs
            tap("pooled", lambda: pooled_tm[:, :], [NLAT, 1024])

            for k in range(KCH):
                reg = ps[:, (k % 2) * 512:(k % 2) * 512 + NLAT]
                T.emit(lambda t, k=k, reg=reg: t.matmul(
                    reg, pooled_tm[0:NLAT, k * 128:(k + 1) * 128],
                    identf_s[0:NLAT, 0:NLAT], start=True, stop=True))
                cT = T.milestone()
                S.wait(T, cT)
                cS = S.emit(lambda s, k=k, reg=reg: s.activation(
                    out=pooled_fmb[:, k, :], in_=reg, func=AF.Identity),
                    inc=True)
                if k % 2 == 1:
                    T.wait(S, cS)
            T.wait(smRkvw, c_rkvw)
            rkv_ev = 0
            for m in range(4):
                reg = ps[:, (2 + m % 2) * 512:(2 + m % 2) * 512 + NLAT]
                T.wait(S, rkv_ev)
                for k in range(KCH):
                    T.emit(lambda t, m=m, k=k, reg=reg: t.matmul(
                        reg, rkvw_sb[:, k, m * 128:(m + 1) * 128],
                        pooled_fmb[:, k, :], start=(k == 0),
                        stop=(k == KCH - 1)))
                cT = T.milestone()
                S.wait(T, cT)
                if m < 2:
                    rkv_ev = S.emit(lambda s, m=m, reg=reg: s.activation(
                        out=rk_fm[:, m, :], in_=reg, func=AF.Identity,
                        bias=rkvb_c[:, m:m + 1]), inc=True)
                else:
                    rkv_ev = S.emit(lambda s, m=m, reg=reg: s.activation(
                        out=rvf[:, m - 2, :], in_=reg, func=AF.Identity,
                        bias=rkvb_c[:, m:m + 1]), inc=True)
            S_rv = rkv_ev
            for ch in range(2):
                T.wait(S, S_rv)
                T.emit(lambda t, ch=ch: t.matmul(
                    ps[0:NLAT, 4 * 512:4 * 512 + 128], rvf[:, ch, :],
                    identb[:], start=True, stop=True))
                cT = T.milestone()
                S.wait(T, cT)
                S_rv = S.emit(lambda s, ch=ch: s.activation(
                    out=rv_tm[0:NLAT, ch, :],
                    in_=ps[0:NLAT, 4 * 512:4 * 512 + 128],
                    func=AF.Identity), inc=True)
            flush_all()
            bar()

        xo = x_own.rearrange("(n p) d -> n p d", p=128)
        nslice_run = 0 if stop_after == "p0" else (
            1 if stop_after in ("attn", "slice0") else NSLICE)
        for s_i in range(nslice_run):
            with contextlib.ExitStack() as ast:
                qkv_sb = sb("qkv_sb", [128, KCH, 3 * D], BF16, ast)
                outw_sb = sb("outw_sb", [128, KCH, D], BF16, ast)
                rqw_sb = sb("rqw_sb", [128, KCH, LD], BF16, ast)
                routw_sb = sb("routw_sb", [128, 2, D], BF16, ast)
                xblk = sb("xblk", [128, 8, 1024], F32, ast)
                lnN = [ktail[:].rearrange("p a b -> p (a b)"), vtail[:]]
                lntm = lnN[0]
                ln1f = sb("ln1f", [128, KCH, BT], BF16, ast)
                qf = sb("qf", [128, KCH, BT], BF16, ast)
                kbuf = sb("kbuf", [128, KCH, 640], BF16, ast)
                vbuf = sb("vbuf", [128, 5, 1024], BF16, ast)
                rqf = sb("rqf", [128, 2, BT], BF16, ast)
                Ebig = sb("Ebig", [128, H, 256], BF16, ast)

                loc_tm = sb("loc_tm", [128, H, HDIM], BF16, ast)
                lat_tm = sb("lat_tm", [128, LH, HDIM], BF16, ast)
                loc_f = sb("loc_f", [128, KCH, BT], BF16, ast)
                lat_f = sb("lat_f", [128, 2, BT], BF16, ast)
                rem_f = sb("rem_f", [128, KCH, BT], BF16, ast)
                mixf = sb("mixf", [128, KCH, BT], BF16, ast)

                ao_t = sb("ao_t", [128, 1024], F32, ast)
                jk = sb("jk", [128, 1024], F32, ast)
                astat_t = sb("astat_t", [128, 32], F32, ast)
                rs_c = sb("rs_c", [128, 8], F32, ast)
                maskT_s = sb("maskT_s", [128, 2, 256], BF16, ast)
                glmat = sb("glmat", [128, KCH, 16], BF16, ast)
                grmat = sb("grmat", [128, KCH, 16], BF16, ast)
                ehm_s = sb("ehm_s", [16, 8, 128], BF16, ast)
                glcol = sb("glcol", [64, 2], BF16, ast)
                gsigb = jk[0:16, 512:768].bitcast(BF16)
                Etmp = jk[:, 768:896].bitcast(BF16)
                Etmp2 = jk[:, 896:1024].bitcast(BF16)
                Elat = ao_t[0:64, 512:768].bitcast(BF16).rearrange(
                    "p (a b) -> p a b", b=128)
                jkv = Ebig[:].rearrange("p a b -> p (a b)").bitcast(F32)
                astats = StatCtl(astat_t, jkv[:, 0:1024])
                sjk = jkv[:, 1024:2048]
                gsig = ao_t[0:16, 0:512]
                ln23 = jk

                QS.wait(smCast, c_wcast)
                QS.dma(lambda q: q.dma_start(
                    out=qkv_sb[:],
                    in_=qkv_wb.rearrange("(a p) b -> p a b", p=128)), smWsl)
                QS.dma(lambda q: q.dma_start(
                    out=outw_sb[:],
                    in_=out_wb.rearrange("(a p) b -> p a b", p=128)), smWsl)
                QS.dma(lambda q: q.dma_start(
                    out=rqw_sb[:],
                    in_=rq_wb.rearrange("(a p) b -> p a b", p=128)), smWsl)
                c_wslice = QS.dma(lambda q: q.dma_start(
                    out=routw_sb[:],
                    in_=rout_wb.rearrange("(a p) b -> p a b", p=128)), smWsl)
                QG.dma(lambda g: g.dma_start(
                    out=maskT_s[:],
                    in_=maskTd[:, :, :].rearrange("a p b -> p a b")), smGl)
                QG.dma(lambda g: g.dma_start(
                    out=ehm_s[:].rearrange("p a b -> p (a b)"),
                    in_=ehmatd[:, :, :].rearrange("p a b -> p (a b)")), smGl)
                QG.dma(lambda g: g.dma_start(out=glcol[0:64, 0:1],
                                             in_=gl_w[:, :]), smGlc)
                cg = QG.dma(lambda g: g.dma_start(out=glcol[0:64, 1:2],
                                                  in_=gr_w[:, :]), smGlc)
                V.wait(smGlc, cg)
                V.emit(lambda v: v.memset(
                    glmat[:].rearrange("p a b -> p (a b)"), 0.0))
                cvz = V.emit(lambda v: v.memset(
                    grmat[:].rearrange("p a b -> p (a b)"), 0.0), inc=True)
                QG.wait(V, cvz)
                QG.wait(smGlc, cg)
                for c in range(KCH):
                    QG.dma(lambda g, c=c: g.dma_start(
                        out=glmat[0:64, c, 2 * c:2 * c + 1],
                        in_=glcol[0:64, 0:1]), smGl)
                    QG.dma(lambda g, c=c: g.dma_start(
                        out=glmat[64:128, c, 2 * c + 1:2 * c + 2],
                        in_=glcol[0:64, 0:1]), smGl)
                    QG.dma(lambda g, c=c: g.dma_start(
                        out=grmat[0:64, c, 2 * c:2 * c + 1],
                        in_=glcol[0:64, 1:2]), smGl)
                    QG.dma(lambda g, c=c: g.dma_start(
                        out=grmat[64:128, c, 2 * c + 1:2 * c + 2],
                        in_=glcol[0:64, 1:2]), smGl)
                flush_all()
                bar()

                if s_i == 0:
                    cq = QS.dma(lambda q: q.dma_start(out=ao_t[:],
                                                      in_=x_halo[:, :]),
                                smHalo)
                    V.wait(smHalo, cq)
                    S.wait(smHalo, cq)
                    si_, r_, nb_ = astats.compute(lambda: ao_t[:], sjk)
                    ch0 = S.emit(lambda s, r_=r_, nb_=nb_: s.activation(
                        out=lntm[:], in_=ao_t[:], func=AF.Identity,
                        bias=nb_, scale=r_), inc=True)
                    astats.consumed(si_, ch0)
                    T.wait(S, ch0)
                    T.wait(smWsl, c_wslice)
                    for k in range(KCH):
                        T.emit(lambda t, k=k: t.matmul(
                            pb(k // 4, (k % 4) * 128, 128),
                            lntm[:, k * 128:(k + 1) * 128], identb[:],
                            start=True, stop=True))
                    cT = T.milestone()
                    V.wait(T, cT)
                    cV = V.emit(lambda v: v.tensor_copy(
                        out=ln1f[:, :, 0:128],
                        in_=pb(0, 0, 1024).rearrange("p (a b) -> p a b", b=128)),
                        inc=True)
                    T.wait(V, cV)
                    ev = 0
                    for m in range(KCH):
                        reg = pb(2 + m % 2, 0, 128)
                        T.wait(S, ev)
                        for k in range(KCH):
                            T.emit(lambda t, m=m, k=k, reg=reg: t.matmul(
                                reg, qkv_sb[:, k, D + m * 128:D + (m + 1) * 128],
                                ln1f[:, k, 0:128], start=(k == 0),
                                stop=(k == KCH - 1)))
                        cT = T.milestone()
                        S.wait(T, cT)
                        ev = S.emit(lambda s, m=m, reg=reg: s.activation(
                            out=ktail[:, m, :], in_=reg, func=AF.Identity,
                            bias=qkvb_c[:, 8 + m:9 + m]), inc=True)
                    T.wait(S, ev)
                    for half in range(2):
                        for k in range(KCH):
                            T.emit(lambda t, k=k, half=half: t.matmul(
                                pb(4 + half, 0, 512), ln1f[:, k, 0:128],
                                qkv_sb[:, k, 2 * D + half * 512:
                                       2 * D + (half + 1) * 512],
                                start=(k == 0), stop=(k == KCH - 1)))
                    cT = T.milestone()
                    S.wait(T, cT)
                    S.emit(lambda s: s.copy(out=vtail[:], in_=pb(4, 0, 1024)),
                           inc=True)
                    flush_all()
                    bar()

                cb = {"attnT": 0,
                      "rm_ev": [0, 0],
                      "og_ev": [0, 0],
                      "stS_ev": [0, 0],
                      "av_sev": [0, 0],
                      "ltT_evS": 0, "ltT_evV": 0,
                      "avT": [0] * H,
                      "latT": [0] * LH,
                      "prev_mask": 0, "locT": 0, "lnG2": 0,
                      "ln3sP": [[0] * 4, [0] * 4],
                      "spxP": [0, 0],
                      "v_ev": [0, 0],
                      "qk_ev": [0, 0],
                      "carry": 0,
                      "lnN_guard": [0, 0],
                      "lnN_evV": 0,
                      "ln3sp": 0,
                      "lnT_evV": 0,
                      "acons": 0,
                      "xadd": 0}

                def emit_carry_A(blk):
                    par = blk % 2
                    cq_xs = []
                    for tc in range(4):
                        QS.wait(S, cb["ln3sP"][par][tc])
                        if tc == 0:
                            QS.wait(smSpill, cb["spxP"][par])
                            QS.wait(S, cb["acons"])
                        cq_xs.append(QS.dma(lambda q, b=blk, s=s_i, par=par,
                                            tc=tc: q.dma_start(
                            out=xblk[:, 4 * par + tc, :],
                            in_=xo[16 * s + 4 * b + tc, :, :]), smX4[tc]))
                    V.wait(T, cb["attnT"])
                    V.wait(S, cb["qk_ev"][0])
                    V.wait(S, cb["qk_ev"][1])
                    V.wait(S, cb["v_ev"][0])
                    V.wait(S, cb["v_ev"][1])
                    if blk == 0:
                        V.emit(lambda v: v.tensor_copy(
                            out=kbuf[:, :, 0:128], in_=ktail[:]))
                        cV_carry = V.emit(lambda v: v.tensor_copy(
                            out=vbuf[:, 0, :], in_=vtail[:]), inc=True)
                    else:
                        V.emit(lambda v: v.tensor_copy(
                            out=kbuf[:, :, 0:128], in_=kbuf[:, :, 512:640]))
                        cV_carry = V.emit(lambda v: v.tensor_copy(
                            out=vbuf[:, 0, :], in_=vbuf[:, 4, :]), inc=True)
                    cb["carry"] = cV_carry
                    S.wait(T, cb["attnT"])
                    st_res = []
                    for tc in range(4):
                        V.wait(smX4[tc], cq_xs[tc])
                        S.wait(smX4[tc], cq_xs[tc])
                        st_res.append(astats.compute(
                            lambda par=par, tc=tc: xblk[:, 4 * par + tc, :],
                            sjk))
                    ln_gv = cb["lnN_evV"]
                    for tc in range(4):
                        si_, r_, nb_ = st_res[tc]
                        S.wait(T, cb["lnN_guard"][tc % 2])
                        if tc < 2:
                            S.wait(V, cb["carry"])
                        cs = S.emit(lambda s, par=par, tc=tc, r_=r_, nb_=nb_:
                                    s.activation(out=lnN[tc % 2],
                                                 in_=xblk[:, 4 * par + tc, :],
                                                 func=AF.Identity, bias=nb_,
                                                 scale=r_), inc=True)
                        astats.consumed(si_, cs)
                        T.wait(S, cs)
                        T.wait(V, ln_gv)
                        if tc == 0:
                            T.wait(S, cb["av_sev"][0])
                            T.wait(S, cb["av_sev"][1])
                            T.wait(S, cb["v_ev"][0])
                            T.wait(S, cb["v_ev"][1])
                        if tc == 1:
                            T.wait(S, cb["ltT_evS"])
                        b0 = 2 * (tc % 2)
                        for k in range(KCH):
                            T.emit(lambda t, k=k, b0=b0, tc=tc: t.matmul(
                                pb(b0 + k // 4, (k % 4) * 128, 128),
                                lnN[tc % 2][:, k * 128:(k + 1) * 128],
                                identb[:], start=True, stop=True))
                        cT = T.milestone()
                        cb["lnN_guard"][tc % 2] = cT
                        V.wait(T, cT)
                        cV = V.emit(lambda v, tc=tc, b0=b0: v.tensor_copy(
                            out=ln1f[:, :, tc * 128:(tc + 1) * 128],
                            in_=pb(b0, 0, 1024).rearrange("p (a b) -> p a b",
                                                          b=128)), inc=True)
                        ln_gv = cV
                    cb["lnN_evV"] = ln_gv
                    cb["acons"] = cs

                def emit_B(blk):
                    T.wait(V, cb["lnN_evV"])
                    qk_ev = [cb["rm_ev"][0], cb["rm_ev"][1]]
                    for m in range(18):
                        par = m % 2
                        reg = pb(4 + par)
                        T.wait(S, qk_ev[par])
                        if m < 2:
                            S.wait(T, cb["attnT"])
                            S.wait(V, cb["carry"])
                        for k in range(KCH):
                            if m < 16:
                                T.emit(lambda t, m=m, k=k, reg=reg: t.matmul(
                                    reg, qkv_sb[:, k, m * 128:(m + 1) * 128],
                                    ln1f[:, k, :], start=(k == 0),
                                    stop=(k == KCH - 1)))
                            else:
                                T.emit(lambda t, m=m, k=k, reg=reg: t.matmul(
                                    reg, rqw_sb[:, k,
                                                (m - 16) * 128:(m - 15) * 128],
                                    ln1f[:, k, :], start=(k == 0),
                                    stop=(k == KCH - 1)))
                        cT = T.milestone()
                        S.wait(T, cT)
                        if m < 8:
                            cS = S.emit(lambda s, m=m, reg=reg: s.activation(
                                out=qf[:, m, :], in_=reg, func=AF.Identity,
                                scale=0.125, bias=qkvbq_c[:, m:m + 1]),
                                inc=True)
                        elif m < 16:
                            cS = S.emit(lambda s, m=m, reg=reg: s.activation(
                                out=kbuf[:, m - 8, 128:640], in_=reg,
                                func=AF.Identity, bias=qkvb_c[:, m:m + 1]),
                                inc=True)
                        else:
                            cS = S.emit(lambda s, m=m, reg=reg: s.activation(
                                out=rqf[:, m - 16, :], in_=reg,
                                func=AF.Identity, scale=0.125,
                                bias=rqbq_c[:, m - 16:m - 15]), inc=True)
                        qk_ev[par] = cS
                    cb["qk_ev"] = qk_ev

                def emit_C(blk, og_ev):
                    v_ev = [cb["v_ev"][0], cb["v_ev"][1]]
                    for tc in range(4):
                        par = tc % 2
                        T.wait(S, v_ev[par])
                        if tc < 2:
                            T.wait(V, og_ev[par])
                        for half in range(2):
                            for k in range(KCH):
                                T.emit(lambda t, tc=tc, k=k, half=half,
                                       par=par: t.matmul(
                                    pb(2 * par + half, 0, 512),
                                    ln1f[:, k, tc * 128:(tc + 1) * 128],
                                    qkv_sb[:, k, 2 * D + half * 512:
                                           2 * D + (half + 1) * 512],
                                    start=(k == 0), stop=(k == KCH - 1)))
                        cT = T.milestone()
                        S.wait(T, cT)
                        cS = S.emit(lambda s, tc=tc, par=par: s.copy(
                            out=vbuf[:, 1 + tc, :], in_=pb(2 * par, 0, 1024)),
                            inc=True)
                        v_ev[par] = cS
                    cb["v_ev"] = v_ev

                emit_carry_A(0)
                emit_B(0)
                emit_C(0, [0, 0])
                flush_all()

                for b_i in range(NBLK):
                    stS_ev = cb["stS_ev"]
                    av_sev = cb["av_sev"]
                    av_vev = [0, 0]
                    T.wait(S, cb["qk_ev"][0])
                    T.wait(S, cb["qk_ev"][1])
                    T.wait(S, cb["v_ev"][0])
                    T.wait(S, cb["v_ev"][1])
                    T.wait(V, cb["carry"])
                    T.wait(V, cb["lnT_evV"])
                    S.wait(smSpillL, cb["ln3sp"])
                    stS_ev[0] = max(stS_ev[0], 0)
                    prev_scnt = cb.get("scnt3", [])
                    prev_ecnt = []
                    for j in range(NSUB):
                        scnt_cur = []
                        ecnt_cur = []
                        if j == 0:
                            T.wait(S, max(cb["qk_ev"][0], cb["qk_ev"][1]))
                            for c_ in cb.get("gcnts", [])[-8:]:
                                T.wait(S, c_)
                                dummy_mm(5)
                        else:
                            for c_ in prev_ecnt[7:15:2]:
                                T.wait(S, c_)
                                dummy_mm(5)
                        prev_scnt = scnt_cur
                        prev_ecnt = ecnt_cur
                        c_loc = s_i * 16 + b_i * 4 + j
                        first = (s_i == 0 and b_i == 0 and j == 0)
                        msel = 1 if first else 0
                        mask_cnt = [0] * H

                        def st_mms(h):
                            ch = h // 2
                            o64 = 64 * (h % 2)
                            r2 = h % 2
                            T.wait(S, stS_ev[r2])
                            for half in range(2):
                                T.emit(lambda t, h=h, half=half, j=j, ch=ch,
                                       o64=o64, r2=r2: t.matmul(
                                    ps[:, (6 + r2) * 512 + half * 128:
                                       (6 + r2) * 512 + half * 128 + 128],
                                    kbuf[o64:o64 + 64, ch,
                                         j * 128 + half * 128:
                                         j * 128 + half * 128 + 128],
                                    qf[o64:o64 + 64, ch,
                                       j * 128:(j + 1) * 128],
                                    start=True, stop=True))
                            return T.milestone()

                        def av_mms(h):
                            r4 = h % 2
                            reg = ps[:, r4 * 512:r4 * 512 + 72]
                            T.wait(S, av_sev[r4])
                            T.wait(V, mask_cnt[h])
                            if j == 0 and h < 2:
                                T.wait(V, cb["og_ev"][r4])
                            for half in range(2):
                                T.emit(lambda t, h=h, j=j, half=half, reg=reg:
                                       t.matmul(
                                    reg[:, 0:64],
                                    Ebig[:, h, half * 128:(half + 1) * 128],
                                    vbuf[:, j + half, h * 64:(h + 1) * 64],
                                    start=(half == 0), stop=(half == 1)))
                            for half in range(2):
                                T.emit(lambda t, h=h, half=half, reg=reg:
                                       t.matmul(
                                    reg[:, 64:65],
                                    Ebig[:, h, half * 128:(half + 1) * 128],
                                    ones_b[:], start=(half == 0),
                                    stop=(half == 1)))
                            cT = T.milestone()
                            cb["avT"][h] = cT
                            V.wait(T, cT)
                            V.wait(S, av_sev[r4])
                            c0 = V.emit(lambda v, reg=reg, r4=r4:
                                        v.tensor_scalar(
                                out=rs_c[:, r4:r4 + 1], in0=reg[:, 64:65],
                                scalar1=1e-30, scalar2=None, op0=ALU.max),
                                inc=True)
                            V.wait(V, c0)
                            cV = V.emit(lambda v, r4=r4: v.reciprocal(
                                out=rs_c[:, r4:r4 + 1],
                                in_=rs_c[:, r4:r4 + 1]), inc=True)
                            av_vev[r4] = cV
                            S.wait(V, cV)
                            S.wait(T, cb["locT"])
                            cS = S.emit(lambda s, h=h, reg=reg, r4=r4:
                                        s.activation(
                                out=loc_tm[:, h, :], in_=reg[:, 0:64],
                                func=AF.Identity,
                                scale=rs_c[:, r4:r4 + 1]), inc=True)
                            av_sev[r4] = cS
                            scnt_cur.append(cS)

                        for h in range(H):
                            cT = st_mms(h)
                            S.wait(T, cT)
                            S.wait(V, mask_cnt[h - 2] if h >= 2
                                   else cb["prev_mask"])
                            et = Etmp[:] if h % 2 == 0 else Etmp2[:]
                            cSe = S.emit(lambda s, r2=h % 2, et=et:
                                         s.activation(
                                out=et,
                                in_=ps[:, (6 + r2) * 512:(6 + r2) * 512 + 256],
                                func=AF.Exp), inc=True)
                            stS_ev[h % 2] = cSe
                            ecnt_cur.append(cSe)
                            V.wait(S, cSe)
                            V.wait(T, cb["avT"][h])
                            cVm = V.emit(lambda v, h=h, msel=msel, et=et:
                                         v.tensor_tensor(
                                out=Ebig[:, h, :], in0=et,
                                in1=maskT_s[:, msel, :], op=ALU.mult),
                                inc=True)
                            mask_cnt[h] = cVm
                            if h >= 1:
                                av_mms(h - 1)
                        av_mms(H - 1)

                        for lh in range(LH):
                            chl = lh // 2
                            o64 = 64 * (lh % 2)
                            rl = lh % 2
                            T.wait(S, stS_ev[rl])
                            T.emit(lambda t, lh=lh, j=j, chl=chl, o64=o64,
                                   rl=rl: t.matmul(
                                ps[0:NLAT, (6 + rl) * 512:(6 + rl) * 512 + 128],
                                rk_fm[o64:o64 + 64, chl, :],
                                rqf[o64:o64 + 64, chl, j * 128:(j + 1) * 128],
                                start=True, stop=True))
                            cT = T.milestone()
                            S.wait(T, cT)
                            cSe = S.emit(lambda s, lh=lh, rl=rl, c=c_loc:
                                         s.activation(
                                out=Elat[0:NLAT, lh, :],
                                in_=ps[0:NLAT, (6 + rl) * 512:
                                       (6 + rl) * 512 + 128],
                                func=AF.Exp, bias=ml_s[0:NLAT, c:c + 1]),
                                inc=True)
                            stS_ev[rl] = cSe
                            scnt_cur.append(cSe)
                            r4 = lh % 2
                            reg = ps[:, r4 * 512:r4 * 512 + 72]
                            T.wait(S, av_sev[r4])
                            T.wait(S, cSe)
                            T.emit(lambda t, lh=lh, reg=reg, chl=chl, o64=o64:
                                   t.matmul(
                                reg[:, 0:64], Elat[0:NLAT, lh, :],
                                rv_tm[0:NLAT, chl, o64:o64 + 64],
                                start=True, stop=True))
                            T.emit(lambda t, lh=lh, reg=reg: t.matmul(
                                reg[:, 64:65], Elat[0:NLAT, lh, :],
                                ones_b[0:NLAT, :], start=True, stop=True))
                            cT = T.milestone()
                            V.wait(T, cT)
                            V.wait(S, av_sev[r4])
                            c0 = V.emit(lambda v, reg=reg, r4=r4:
                                        v.tensor_scalar(
                                out=rs_c[:, 4 + r4:5 + r4], in0=reg[:, 64:65],
                                scalar1=1e-30, scalar2=None, op0=ALU.max),
                                inc=True)
                            V.wait(V, c0)
                            cV = V.emit(lambda v, r4=r4: v.reciprocal(
                                out=rs_c[:, 4 + r4:5 + r4],
                                in_=rs_c[:, 4 + r4:5 + r4]), inc=True)
                            S.wait(V, cV)
                            S.wait(T, cb["locT"])
                            cS = S.emit(lambda s, lh=lh, reg=reg, r4=r4:
                                        s.activation(
                                out=lat_tm[:, lh, :], in_=reg[:, 0:64],
                                func=AF.Identity,
                                scale=rs_c[:, 4 + r4:5 + r4]), inc=True)
                            av_sev[r4] = cS
                            scnt_cur.append(cS)
                        cb["prev_mask"] = mask_cnt[H - 1]
                        flush_all()

                        T.wait(S, av_sev[0])
                        T.wait(S, av_sev[1])
                        T.wait(S, cb["ltT_evS"])
                        T.wait(V, cb["ltT_evV"])
                        for k in range(KCH):
                            T.emit(lambda t, k=k: t.matmul(
                                pb(2 + k // 4, (k % 4) * 128, 128),
                                loc_tm[:].rearrange("p a b -> p (a b)")[
                                    :, k * 128:(k + 1) * 128],
                                identb[:], start=True, stop=True))
                        for ch2 in range(2):
                            T.emit(lambda t, ch2=ch2: t.matmul(
                                pb(4, ch2 * 128, 128),
                                lat_tm[:].rearrange("p a b -> p (a b)")[
                                    :, ch2 * 128:(ch2 + 1) * 128],
                                identb[:], start=True, stop=True))
                        cT = T.milestone()
                        S.wait(T, cT)
                        for k in range(KCH):
                            S.emit(lambda s, k=k, j=j: s.activation(
                                out=loc_f[:, k, j * 128:(j + 1) * 128],
                                in_=pb(2 + k // 4, (k % 4) * 128, 128),
                                func=AF.Identity,
                                bias=qkvb_c[:, 16 + k:17 + k]), inc=True)
                        cb["locT"] = cT
                        V.wait(T, cT)
                        for ch2 in range(2):
                            cVl = V.emit(lambda v, ch2=ch2, j=j: v.tensor_copy(
                                out=lat_f[:, ch2, j * 128:(j + 1) * 128],
                                in_=pb(4, ch2 * 128, 128)), inc=True)
                        cb["ltT_evS"] = cS
                        cb["ltT_evV"] = cVl
                        flush_all()
                    cb["scnt3"] = list(prev_scnt)
                    for c_ in prev_scnt[-14::2]:
                        T.wait(S, c_)
                        dummy_mm(5)
                    cb["attnT"] = T.n
                    tap("loc_f_%d_%d" % (s_i, b_i),
                        lambda: loc_f[:].rearrange("p a b -> p (a b)"),
                        [128, KCH * BT], BF16)
                    tap("lat_f_%d_%d" % (s_i, b_i),
                        lambda: lat_f[:].rearrange("p a b -> p (a b)"),
                        [128, 2 * BT], BF16)

                    rm_ev = [cb["qk_ev"][0], cb["qk_ev"][1]]
                    T.wait(V, cb["ltT_evV"])
                    for m in range(KCH):
                        par = m % 2
                        reg = pb(4 + par)
                        T.wait(S, rm_ev[par])
                        for k in range(2):
                            T.emit(lambda t, m=m, k=k, reg=reg: t.matmul(
                                reg, routw_sb[:, k, m * 128:(m + 1) * 128],
                                lat_f[:, k, :], start=(k == 0), stop=(k == 1)))
                        cT = T.milestone()
                        S.wait(T, cT)
                        cS = S.emit(lambda s, m=m, reg=reg: s.activation(
                            out=rem_f[:, m, :], in_=reg, func=AF.Identity,
                            bias=routb_c[:, m:m + 1]), inc=True)
                        rm_ev[par] = cS
                    cb["rm_ev"] = rm_ev
                    flush_all()
                    tap("rem_f_%d_%d" % (s_i, b_i),
                        lambda: rem_f[:].rearrange("p a b -> p (a b)"),
                        [128, KCH * BT], BF16)

                    if b_i + 1 < NBLK:
                        emit_carry_A(b_i + 1)

                    T.wait(S, cb["ltT_evS"])
                    T.wait(S, rm_ev[0])
                    T.wait(S, rm_ev[1])
                    T.wait(S, stS_ev[0])
                    T.wait(S, stS_ev[1])
                    for k in range(KCH):
                        T.emit(lambda t, k=k: t.matmul(
                            ps[0:16, 6 * 512:6 * 512 + 512], glmat[:, k, :],
                            loc_f[:, k, :], start=(k == 0), stop=False))
                    for k in range(KCH):
                        T.emit(lambda t, k=k: t.matmul(
                            ps[0:16, 6 * 512:6 * 512 + 512], grmat[:, k, :],
                            rem_f[:, k, :], start=False, stop=(k == KCH - 1)))
                    cT = T.milestone()
                    S.wait(T, cT)
                    cS = S.emit(lambda s: s.activation(
                        out=gsig[0:16, :], in_=ps[0:16, 6 * 512:6 * 512 + 512],
                        func=AF.Exp, scale=-1.0), inc=True)
                    V.wait(S, cS)
                    c0 = V.emit(lambda v: v.tensor_scalar(
                        out=gsig[0:16, :], in0=gsig[0:16, :], scalar1=1.0,
                        scalar2=None, op0=ALU.add), inc=True)
                    V.wait(V, c0)
                    c0 = V.emit(lambda v: v.reciprocal(out=gsig[0:16, :],
                                                       in_=gsig[0:16, :]),
                                inc=True)
                    V.wait(V, c0)
                    cVg = V.emit(lambda v: v.tensor_copy(out=gsigb[0:16, :],
                                                         in_=gsig[0:16, :]),
                                 inc=True)
                    gb_ev = [cVg, cVg]
                    if b_i + 1 < NBLK:
                        emit_B(b_i + 1)
                    prev_mix = 0
                    for c in range(KCH):
                        gpar = c % 2
                        T.wait(V, gb_ev[gpar])
                        T.emit(lambda t, c=c, gpar=gpar: t.matmul(
                            pb(6 + gpar), ehm_s[:, c, :], gsigb[0:16, :],
                            start=True, stop=True))
                        cT = T.milestone()
                        V.wait(T, cT)
                        V.wait(V, prev_mix)
                        c0 = V.emit(lambda v, c=c: v.tensor_tensor(
                            out=jk[:, 0:512], in0=loc_f[:, c, :],
                            in1=rem_f[:, c, :], op=ALU.subtract), inc=True)
                        V.wait(V, c0)
                        c0 = V.emit(lambda v, c=c, gpar=gpar: v.tensor_tensor(
                            out=jk[:, 0:512], in0=jk[:, 0:512], in1=pb(6 + gpar),
                            op=ALU.mult), inc=True)
                        V.wait(V, c0)
                        gb_ev[gpar] = V.emit(lambda v, c=c: v.tensor_tensor(
                            out=mixf[:, c, :], in0=jk[:, 0:512],
                            in1=rem_f[:, c, :], op=ALU.add), inc=True)
                        prev_mix = gb_ev[gpar]
                    flush_all()
                    tap("mixf_%d_%d" % (s_i, b_i),
                        lambda: mixf[:].rearrange("p a b -> p (a b)"),
                        [128, KCH * BT], BF16)

                    og_ev = [0, 0]
                    ao_guard = 0
                    par_b = b_i % 2
                    T.wait(V, prev_mix)
                    T.wait(V, cb["lnN_evV"])
                    ln3_apply = [0, 0, 0, 0]
                    lnT_cT = [0, 0, 0, 0]
                    gb_guard = prev_mix
                    c_spx = 0
                    lnst = {"sp": 0, "evV": cb["lnT_evV"]}
                    gcnts = []
                    ltmf = loc_tm[:].rearrange("p a b -> p (a b)")

                    def emit_lnT(tc):
                        T.wait(S, ln3_apply[tc])
                        if tc == 0:
                            T.wait(V, gb_guard)
                        else:
                            T.wait(V, lnst["evV"])
                        for k in range(KCH):
                            T.emit(lambda t, k=k, tc=tc: t.matmul(
                                pb(6 + k // 4, (k % 4) * 128, 128),
                                lnN[tc % 2][:, k * 128:(k + 1) * 128],
                                identb[:], start=True, stop=True))
                        cT = T.milestone()
                        lnT_cT[tc] = cT
                        V.wait(T, cT)
                        V.wait(smSpillL, lnst["sp"])
                        lnst["evV"] = V.emit(lambda v: v.tensor_copy(
                            out=ltmf[:, :], in_=pb(6, 0, 1024)), inc=True)
                        QG.wait(V, lnst["evV"])
                        lnst["sp"] = QG.dma(lambda g, s=s_i, b=b_i, tc=tc:
                                            g.dma_start(
                            out=ln3_d[s, :, :,
                                      b * 512 + tc * 128:
                                      b * 512 + (tc + 1) * 128],
                            in_=ltmf[:, :].rearrange("p (a b) -> p a b",
                                                     b=128)), smSpillL)

                    for tc in range(4):
                        par = tc % 2
                        slot = 4 * par_b + tc
                        T.wait(V, og_ev[par])
                        for half in range(2):
                            T.emit(lambda t, half=half, par=par: t.matmul(
                                pb(2 * par + half, 0, 512),
                                onesrow_b[0:1, :],
                                obrow2[0:1, half * 512:(half + 1) * 512],
                                start=True, stop=False))
                            T.emit(lambda t, half=half, par=par: t.matmul(
                                pb(2 * par + half, 0, 512),
                                onesrow_b[0:1, :],
                                obrow2[0:1, D + half * 512:
                                       D + (half + 1) * 512],
                                start=False, stop=False))
                            for k in range(KCH):
                                T.emit(lambda t, tc=tc, k=k, half=half,
                                       par=par: t.matmul(
                                    pb(2 * par + half, 0, 512),
                                    mixf[:, k, tc * 128:(tc + 1) * 128],
                                    outw_sb[:, k, half * 512:(half + 1) * 512],
                                    start=False, stop=(k == KCH - 1)))
                        cT = T.milestone()
                        if tc >= 2:
                            emit_lnT(tc - 2)
                        V.wait(T, cT)
                        V.wait(S, ao_guard)
                        cV = V.emit(lambda v, par=par: v.tensor_copy(
                            out=ao_t[:], in_=pb(2 * par, 0, 1024)), inc=True)
                        og_ev[par] = cV
                        S.wait(V, cV)
                        V.wait(V, cV)
                        si_, r_, nb_ = astats.compute(lambda: ao_t[:], sjk)
                        cs = S.emit(lambda s, r_=r_, nb_=nb_: s.activation(
                            out=ln23[:, 0:1024], in_=ao_t[:],
                            func=AF.Identity, bias=nb_, scale=r_), inc=True)
                        astats.consumed(si_, cs)
                        ao_guard = cs
                        gcnts.append(cs)
                        V.wait(S, cs)
                        cV = V.emit(lambda v, slot=slot: v.tensor_tensor(
                            out=xblk[:, slot, :], in0=xblk[:, slot, :],
                            in1=ln23[:, 0:1024], op=ALU.add), inc=True)
                        QG.wait(V, cV)
                        c_spx = QG.dma(lambda g, s=s_i, b=b_i, slot=slot:
                                       g.dma_start(
                            out=x2_d[s, 4 * b + slot - 4 * (b % 2)],
                            in_=xblk[:, slot, :]), smSpill)
                        S.wait(V, cV)
                        V.wait(V, cV)
                        si_, r_, nb_ = astats.compute(
                            lambda slot=slot: xblk[:, slot, :], sjk)
                        if tc <= 1:
                            S.wait(T, cb["lnN_guard"][tc % 2])
                        else:
                            S.wait(T, lnT_cT[tc - 2])
                        cs = S.emit(lambda s, slot=slot, tc=tc, r_=r_, nb_=nb_:
                                    s.activation(
                                        out=lnN[tc % 2],
                                        in_=xblk[:, slot, :],
                                        func=AF.Identity, bias=nb_,
                                        scale=r_), inc=True)
                        astats.consumed(si_, cs)
                        ln3_apply[tc] = cs
                        gcnts.append(cs)
                    cb["gcnts"] = gcnts
                    cb["og_ev"] = og_ev
                    cb["xadd"] = cV
                    cb["ln3sP"][par_b] = list(ln3_apply)
                    cb["spxP"][par_b] = c_spx

                    if b_i + 1 < NBLK:
                        emit_C(b_i + 1, og_ev)

                    emit_lnT(2)
                    emit_lnT(3)
                    cb["lnT_evV"] = lnst["evV"]
                    cb["ln3sp"] = lnst["sp"]
                    cb["lnG2"] = lnT_cT[3]
                    cb["lnN_guard"][0] = lnT_cT[2]
                    cb["lnN_guard"][1] = lnT_cT[3]
                    flush_all()
                    tap("x2_%d_%d" % (s_i, b_i),
                        lambda par_b=par_b: xblk[:, 4 * par_b:4 * par_b + 4, :]
                        .rearrange("p a b -> p (a b)"),
                        [128, 4 * 1024])
                V.wait(T, cb["lnG2"])
                V.wait(T, cb["attnT"])
                V.wait(S, cb["qk_ev"][0])
                V.wait(S, cb["qk_ev"][1])
                V.wait(S, cb["v_ev"][0])
                V.wait(S, cb["v_ev"][1])
                V.emit(lambda v: v.tensor_copy(
                    out=ktail[:], in_=kbuf[:, :, 512:640]))
                V.emit(lambda v: v.tensor_copy(out=vtail[:], in_=vbuf[:, 4, :]),
                       inc=True)
                flush_all()
                bar()
            if stop_after == "attn":
                continue
            with contextlib.ExitStack() as fst:
                e8_s = sb("e8_s", [8, 8, 128], BF16, fst)
                b2b = sb("b2b", [8, 1024], BF16, fst)
                QS.dma(lambda q: q.dma_start(
                    out=e8_s[:].rearrange("p a b -> p (a b)"),
                    in_=e8d[:, :, :].rearrange("p a b -> p (a b)")), smConstA)
                c_e8 = smConstA.n
                c_b2b = QG.dma(lambda g: g.dma_start(out=b2b[:, :],
                                                     in_=b2[:, :]), smB2b)
                T.wait(smConstA, c_e8)
                V.wait(smConstA, c_e8)
                S.wait(smConstA, c_e8)
                T.wait(smB2b, c_b2b)
                V.wait(smB2b, c_b2b)
                S.wait(smB2b, c_b2b)
                ln3f = sb("ln3f", [128, KCH, TSLICE], BF16, fst)
                y_tm = sb("y_tm", [128, 16, 1024], F32, fst)
                h2 = sb("h2", [128, 8, TSLICE], BF16, fst)
                gdf = sb("gdf", [8, TSLICE], BF16, fst)
                fstat = sb("fstat", [128, 16], F32, fst)
                fwst = fst.enter_context(contextlib.ExitStack())
                hmul = sb("hmul", [128, TSLICE], BF16, fwst)
                sa_t = sb("sa_t", [128, TSLICE], BF16, fwst)
                g_sb = sb("g_sb", [128, TSLICE], BF16, fwst)
                wst1 = sb("wst1", [128, 2, 2, 8, 128], BF16, fwst)
                wst3 = sb("wst3", [128, 2, 2, 8, 128], BF16, fwst)
                w2t_sb = sb("w2t_sb", [128, 8, 1024], BF16, fwst)
                glog = sb("glog", [8, 512], F32, fwst)
                gdt = sb("gdt", [128, 8], F32, fwst)
                gsc1 = sb("gsc1", [128, 8], F32, fwst)
                gdtb = sb("gdtb", [128, 8], BF16, fwst)

                cq_ln3 = QS.dma(lambda q, s=s_i: q.dma_start(
                    out=ln3f[:].rearrange("p a b -> p (a b)"),
                    in_=ln3_d[s].rearrange("p a b -> p (a b)")), smLn3)

                V.wait(smLn3, cq_ln3)
                T.wait(smLn3, cq_ln3)
                gl_ev = 0
                gdT_ev = 0
                glogT_used = 0
                for tb in range(4):
                    T.wait(S, gl_ev)
                    for k in range(KCH):
                        T.emit(lambda t, k=k, tb=tb: t.matmul(
                            ps[0:8, 6 * 512:6 * 512 + 512], gate_wT[:, k, :],
                            ln3f[:, k, tb * 512:(tb + 1) * 512],
                            start=(k == 0), stop=(k == KCH - 1)))
                    cT = T.milestone()
                    S.wait(T, cT)
                    S.wait(T, glogT_used)
                    gl_ev = S.emit(lambda s: s.activation(
                        out=glog[0:8, :], in_=ps[0:8, 6 * 512:6 * 512 + 512],
                        func=AF.Identity, bias=gateb_c[0:8, 0:1]), inc=True)
                    for tci in range(4):
                        tc = tb * 4 + tci
                        T.wait(S, gl_ev)
                        T.wait(V, gdT_ev)
                        T.emit(lambda t, tci=tci: t.matmul(
                            ps[:, 7 * 512:7 * 512 + 8],
                            glog[0:8, tci * 128:(tci + 1) * 128],
                            identf_s[0:8, 0:8], start=True, stop=True))
                        cT = T.milestone()
                        glogT_used = cT
                        V.wait(T, cT)
                        c0 = V.emit(lambda v: v.reduce_max(
                            out=fstat[:, 0:1], in_=ps[:, 7 * 512:7 * 512 + 8],
                            axis=AX.X, negate=True), inc=True)
                        cV = V.emit(lambda v: v.tensor_copy(
                            out=gdt[:, :], in_=ps[:, 7 * 512:7 * 512 + 8]),
                            inc=True)
                        gdT_ev = cV
                        S.wait(V, cV)
                        cS = S.emit(lambda s: s.activation(
                            out=gsc1[:, :], in_=gdt[:, :], func=AF.Exp,
                            bias=fstat[:, 0:1],
                            accum_out=fstat[:, 1:2]), inc=True)
                        V.wait(S, cS)
                        c0 = V.emit(lambda v: v.reciprocal(out=fstat[:, 2:3],
                                                           in_=fstat[:, 1:2]),
                                    inc=True)
                        V.wait(V, c0)
                        c0 = V.emit(lambda v: v.tensor_scalar(
                            out=gsc1[:, :], in0=gsc1[:, :],
                            scalar1=fstat[:, 2:3], scalar2=None, op0=ALU.mult),
                            inc=True)
                        V.wait(V, c0)
                        c0 = V.emit(lambda v: v.reduce_max(
                            out=fstat[:, 3:4], in_=gsc1[:, :], axis=AX.X),
                            inc=True)
                        V.wait(V, c0)
                        c0 = V.emit(lambda v: v.tensor_scalar(
                            out=gdt[:, :], in0=gsc1[:, :],
                            scalar1=fstat[:, 3:4], scalar2=None,
                            op0=ALU.is_ge), inc=True)
                        V.wait(V, c0)
                        c0 = V.emit(lambda v: v.scalar_tensor_tensor(
                            out=gdt[:, :], in0=gdt[:, :], scalar=-1e30,
                            in1=gsc1[:, :], op0=ALU.mult, op1=ALU.add),
                            inc=True)
                        V.wait(V, c0)
                        c0 = V.emit(lambda v: v.reduce_max(
                            out=fstat[:, 4:5], in_=gdt[:, :], axis=AX.X),
                            inc=True)
                        V.wait(V, c0)
                        c0 = V.emit(lambda v: v.tensor_scalar(
                            out=gdt[:, :], in0=gsc1[:, :],
                            scalar1=fstat[:, 4:5], scalar2=None,
                            op0=ALU.is_ge), inc=True)
                        V.wait(V, c0)
                        c0 = V.emit(lambda v: v.tensor_tensor(
                            out=gdt[:, :], in0=gdt[:, :], in1=gsc1[:, :],
                            op=ALU.mult), inc=True)
                        V.wait(V, c0)
                        cV = V.emit(lambda v: v.tensor_copy(out=gdtb[:, :],
                                                            in_=gdt[:, :]),
                                    inc=True)
                        T.wait(V, cV)
                        T.emit(lambda t: t.matmul(
                            ps[0:8, 7 * 512 + 128:7 * 512 + 256],
                            gdtb[:, :], identb[:], start=True, stop=True))
                        cT = T.milestone()
                        V.wait(T, cT)
                        cV = V.emit(lambda v, tc=tc: v.tensor_copy(
                            out=gdf[0:8, tc * 128:(tc + 1) * 128],
                            in_=ps[0:8, 7 * 512 + 128:7 * 512 + 256]),
                            inc=True)
                        gdT_ev = cV
                flush_all()
                bar()
                tap("gdf_%d" % s_i, lambda: gdf[0:8, :], [8, TSLICE], BF16)

                fjobs = [("sh", 0), ("sh", 1)] + [("ex", e) for e in range(EXP)]
                wld = 0
                prev_aev = 0
                prev_vev = 0
                prev_yev = 0
                prev_yT = 0
                prev_wldu = [0, 0]
                for jkind, jidx in fjobs:
                    for grp in range(4):
                        QS.wait(T, prev_wldu[grp % 2])
                        if jkind == "sh":
                            mbase = jidx * 8 + grp * 2
                            src1 = ws1T[:, mbase:mbase + 2, :, :]
                            src3 = ws3T[:, mbase:mbase + 2, :, :]
                        else:
                            mbase = grp * 2
                            src1 = w1T[jidx, :, mbase:mbase + 2, :, :]
                            src3 = w3T[jidx, :, mbase:mbase + 2, :, :]
                        QS.dma(lambda q, s1=src1, g=grp: q.dma_start(
                            out=wst1[:, g % 2].rearrange("p a b c -> p (a b c)"),
                            in_=s1.rearrange("p a b c -> p (a b c)")),
                            smFw[grp % 2])
                        wld = QS.dma(lambda q, s3=src3, g=grp: q.dma_start(
                            out=wst3[:, g % 2].rearrange("p a b c -> p (a b c)"),
                            in_=s3.rearrange("p a b c -> p (a b c)")),
                            smFw[grp % 2])
                        if grp == 0:
                            wld0 = wld
                        if grp == 1:
                            QS.wait(T, prev_yT)
                            if jkind == "sh":
                                c_w2t = QS.dma(lambda q, hf=jidx: q.dma_start(
                                    out=w2t_sb[:].rearrange("p a b -> p (a b)"),
                                    in_=ws2T[:, hf * 8:(hf + 1) * 8, :]
                                    .rearrange("p a b -> p (a b)")), smW2t)
                            else:
                                c_w2t = QS.dma(lambda q, e=jidx: q.dma_start(
                                    out=w2t_sb[:].rearrange("p a b -> p (a b)"),
                                    in_=w2T[e].rearrange("p a b -> p (a b)")),
                                    smW2t)
                        if grp == 1:
                            break
                    if jkind == "ex":
                        T.wait(S, prev_aev)
                        for tb in range(4):
                            T.emit(lambda t, e=jidx, tb=tb: t.matmul(
                                pb(4 + tb), e8_s[:, e, :],
                                gdf[0:8, tb * 512:(tb + 1) * 512],
                                start=True, stop=True))
                        cT = T.milestone()
                        S.wait(T, cT)
                        S.wait(V, prev_vev)
                        cS_g = S.emit(lambda s: s.copy(out=g_sb[:],
                                                       in_=pb(4, 0, 2048)),
                                      inc=True)
                    else:
                        cS_g = 0
                    flush_all()

                    T.wait(V, prev_yev)
                    T.wait(S, prev_aev)
                    bev = 0
                    aev = prev_aev
                    vev = prev_vev
                    wld_grp = [wld0, wld, 0, 0]
                    wld_used = [0, 0]
                    for mi in range(8):
                        gpar = (mi // 2) % 2
                        if mi >= 2 and mi % 2 == 0:
                            grp = mi // 2 + 1
                            if grp < 4:
                                if jkind == "sh":
                                    mb = jidx * 8 + grp * 2
                                    s1 = ws1T[:, mb:mb + 2, :, :]
                                    s3 = ws3T[:, mb:mb + 2, :, :]
                                else:
                                    mb = grp * 2
                                    s1 = w1T[jidx, :, mb:mb + 2, :, :]
                                    s3 = w3T[jidx, :, mb:mb + 2, :, :]
                                QS.wait(T, wld_used[grp % 2])
                                QS.dma(lambda q, s1=s1, g=grp: q.dma_start(
                                    out=wst1[:, g % 2].rearrange(
                                        "p a b c -> p (a b c)"),
                                    in_=s1.rearrange("p a b c -> p (a b c)")),
                                    smFw[grp % 2])
                                wld_grp[grp] = QS.dma(lambda q, s3=s3, g=grp:
                                              q.dma_start(
                                    out=wst3[:, g % 2].rearrange(
                                        "p a b c -> p (a b c)"),
                                    in_=s3.rearrange("p a b c -> p (a b c)")),
                                    smFw[grp % 2])
                        if mi % 2 == 0:
                            T.wait(smFw[(mi // 2) % 2], wld_grp[mi // 2])
                        T.wait(S, bev)
                        for k in range(KCH):
                            for tb in range(4):
                                T.emit(lambda t, mi=mi, k=k, tb=tb, gpar=gpar:
                                       t.matmul(
                                    pb(tb, 0, 512), wst3[:, gpar, mi % 2, k, :],
                                    ln3f[:, k, tb * 512:(tb + 1) * 512],
                                    start=(k == 0), stop=(k == KCH - 1)))
                        cT3 = T.milestone()
                        T.wait(S, aev)
                        T.wait(S, cS_g)
                        for k in range(KCH):
                            for tb in range(4):
                                T.emit(lambda t, mi=mi, k=k, tb=tb, gpar=gpar:
                                       t.matmul(
                                    pb(4 + tb, 0, 512),
                                    wst1[:, gpar, mi % 2, k, :],
                                    ln3f[:, k, tb * 512:(tb + 1) * 512],
                                    start=(k == 0), stop=(k == KCH - 1)))
                        cT1 = T.milestone()
                        if mi % 2 == 1:
                            wld_used[(mi // 2) % 2] = cT1
                        if jkind == "sh":
                            bb3 = bs3_c[:, jidx * 8 + mi:jidx * 8 + mi + 1]
                            bb1 = bs1_c[:, jidx * 8 + mi:jidx * 8 + mi + 1]
                        else:
                            bb3 = b3_c[:, jidx * 8 + mi:jidx * 8 + mi + 1]
                            bb1 = b1_c[:, jidx * 8 + mi:jidx * 8 + mi + 1]
                        S.wait(T, cT3)
                        bev = S.emit(lambda s, mi=mi, bb3=bb3: s.activation(
                            out=h2[:, mi, :], in_=pb(0, 0, 2048),
                            func=AF.Identity, bias=bb3), inc=True)
                        S.wait(T, cT1)
                        S.wait(V, vev)
                        aev = S.emit(lambda s, bb1=bb1: s.activation(
                            out=sa_t[:], in_=pb(4, 0, 2048), func=AF.Silu,
                            bias=bb1), inc=True)
                        V.wait(S, bev)
                        V.wait(V, vev)
                        if jkind == "ex":
                            V.wait(S, cS_g)
                            c0 = V.emit(lambda v, mi=mi: v.tensor_tensor(
                                out=hmul[:], in0=h2[:, mi, :], in1=g_sb[:],
                                op=ALU.mult), inc=True)
                        else:
                            c0 = V.emit(lambda v, mi=mi: v.tensor_copy(
                                out=hmul[:], in_=h2[:, mi, :]), inc=True)
                        V.wait(V, c0)
                        V.wait(S, aev)
                        vev = V.emit(lambda v, mi=mi: v.tensor_tensor(
                            out=h2[:, mi, :], in0=hmul[:], in1=sa_t[:],
                            op=ALU.mult), inc=True)
                    flush_all()

                    T.wait(S, bev)
                    T.wait(V, vev)
                    T.wait(smW2t, c_w2t)
                    yev = [0, 0]
                    for ts in range(16):
                        par = ts % 2
                        T.wait(V, yev[par])
                        for k in range(KCH):
                            for half in range(2):
                                T.emit(lambda t, ts=ts, k=k, half=half,
                                       par=par: t.matmul(
                                    pb(2 * par + half, 0, 512),
                                    h2[:, k, ts * 128:(ts + 1) * 128],
                                    w2t_sb[:, k, half * 512:(half + 1) * 512],
                                    start=(k == 0), stop=False
                                    if k < KCH - 1 else
                                    (jkind != "ex" or jidx != 0)))
                        if jkind == "ex" and jidx == 0:
                            for half in range(2):
                                T.emit(lambda t, ts=ts, half=half, par=par:
                                       t.matmul(
                                    pb(2 * par + half, 0, 512),
                                    gdf[0:8, ts * 128:(ts + 1) * 128],
                                    b2b[0:8, half * 512:(half + 1) * 512],
                                    start=False, stop=True))
                        cT = T.milestone()
                        V.wait(T, cT)
                        if jkind == "sh" and jidx == 0:
                            cV = V.emit(lambda v, ts=ts, par=par:
                                        v.tensor_tensor(
                                out=y_tm[:, ts, :], in0=pb(2 * par, 0, 1024),
                                in1=bs2_bc[:].rearrange("p a b -> p (a b)"),
                                op=ALU.add), inc=True)
                        else:
                            cV = V.emit(lambda v, ts=ts, par=par:
                                        v.tensor_tensor(
                                out=y_tm[:, ts, :], in0=pb(2 * par, 0, 1024),
                                in1=y_tm[:, ts, :], op=ALU.add), inc=True)
                        yev[par] = cV
                    prev_aev = aev
                    prev_vev = vev
                    prev_yev = max(yev[0], yev[1])
                    prev_yT = T.n
                    prev_wldu = [wld_used[0], wld_used[1]]
                    flush_all()
                tap("y_%d" % s_i,
                    lambda: y_tm[:].rearrange("p a b -> p (a b)"),
                    [128, 16 * 1024])
                bar()
                fwst.close()
                x2l = sb("x2l", [128, 2, 1024], F32, fst)
                jk2 = sb("jk2", [128, 1024], F32, fst)
                jk2v = sb("jk2v", [128, 1024], F32, fst)
                fstats = StatCtl(fstat, jk2v[:])

                x2st = [0, 0]
                st_done = 0
                prev_add = 0
                for ts in range(16):
                    par = ts % 2
                    QS.wait(smOut[par], x2st[par])
                    cq = QS.dma(lambda q, s=s_i, ts=ts, par=par: q.dma_start(
                        out=x2l[:, par, :], in_=x2_d[s, ts]), smX2l[par])
                    S.wait(V, prev_add)
                    si_, r_, nb_ = fstats.compute(
                        lambda ts=ts: y_tm[:, ts, :], jk2[:])
                    cs = S.emit(lambda s, ts=ts, r_=r_, nb_=nb_: s.activation(
                        out=jk2[:], in_=y_tm[:, ts, :], func=AF.Identity,
                        bias=nb_, scale=r_), inc=True)
                    fstats.consumed(si_, cs)
                    V.wait(S, cs)
                    V.wait(smX2l[par], cq)
                    cV = V.emit(lambda v, par=par: v.tensor_tensor(
                        out=x2l[:, par, :], in0=x2l[:, par, :], in1=jk2[:],
                        op=ALU.add), inc=True)
                    prev_add = cV
                    QS.wait(V, cV)
                    st_done = QS.dma(lambda q, s=s_i, ts=ts, par=par:
                                     q.dma_start(
                        out=y_out[(s * 16 + ts) * 128:(s * 16 + ts + 1) * 128,
                                  :], in_=x2l[:, par, :]), smOut[par])
                    x2st[par] = st_done
                flush_all()
                bar()
        bar()
        flush_all()
    return nc, dbg_outs


def host_consts(h):
    bf = ml_dtypes.bfloat16
    ident = np.eye(128, dtype=np.float32)
    qi = np.arange(128)[None, :]
    kj = np.arange(128)[:, None]
    mA = (kj >= qi).astype(bf)
    mB = (kj < qi).astype(bf)
    mask_std = np.concatenate([mA, mB], axis=1)
    if h == 0:
        mask_first = np.concatenate([np.zeros_like(mA), mB], axis=1)
    else:
        mask_first = mask_std
    maskT = np.stack([mask_std, mask_first], axis=0)
    lat = np.arange(NLAT)[:, None]
    gchunk = 32 * h + np.arange(32)[None, :]
    ml = np.where(lat < gchunk, 0.0, -1e9).astype(np.float32)
    hh = np.arange(16)[:, None, None]
    cc = np.arange(8)[None, :, None]
    mm = np.arange(128)[None, None, :]
    ehmat = (hh == 2 * cc + mm // 64).astype(bf)
    kk = np.arange(8)[:, None, None]
    ee = np.arange(8)[None, :, None]
    e8 = np.broadcast_to(kk == ee, (8, 8, 128)).astype(bf)
    return {"identf": ident, "maskTd": maskT, "mld": ml,
            "ehmatd": np.ascontiguousarray(ehmat),
            "e8d": np.ascontiguousarray(e8)}


def make_in_maps(inputs):
    f32 = np.float32
    x = np.asarray(inputs["L0_attn_x"], dtype=f32)
    wmap = {
        "qkv_w": "L0_attn_qkv_w", "qkv_b": "L0_attn_qkv_b",
        "rq_w": "L0_attn_rq_w", "rq_b": "L0_attn_rq_b",
        "rkv_w": "L0_attn_rkv_w", "rkv_b": "L0_attn_rkv_b",
        "rout_w": "L0_attn_rout_w", "rout_b": "L0_attn_rout_b",
        "out_w": "L0_attn_out_w", "out_b": "L0_attn_out_b",
        "gl_w": "L0_attn_gl_w", "gr_w": "L0_attn_gr_w",
        "gate_w": "L0_ffn_gate_w", "gate_b": "L0_ffn_gate_b",
        "w1": "L0_ffn_w1", "b1": "L0_ffn_b1", "w3": "L0_ffn_w3",
        "b3": "L0_ffn_b3", "w2": "L0_ffn_w2", "b2": "L0_ffn_b2",
        "ws1": "L0_ffn_ws1", "bs1": "L0_ffn_bs1",
        "ws3": "L0_ffn_ws3", "bs3": "L0_ffn_bs3",
        "ws2": "L0_ffn_ws2", "bs2": "L0_ffn_bs2",
    }
    ws = {k: np.ascontiguousarray(np.asarray(inputs[v], dtype=f32))
          for k, v in wmap.items()}
    consts = [host_consts(0), host_consts(1)]
    zeros_halo = np.zeros((128, D), dtype=f32)
    in_maps = []
    for c in range(8):
        b, h = c // 2, c % 2
        m = dict(ws)
        m.update(consts[h])
        m["x_own"] = np.ascontiguousarray(x[b, h * TCORE:(h + 1) * TCORE])
        m["x_full"] = np.ascontiguousarray(x[b])
        m["x_halo"] = (zeros_halo if h == 0 else
                       np.ascontiguousarray(x[b, TCORE - 128:TCORE]))
        in_maps.append(m)
    return in_maps


_BUILD_CACHE = {}


def get_nc(dbg_key=None):
    key = dbg_key or tuple(sorted(DEBUG.items()))
    if key not in _BUILD_CACHE:
        _BUILD_CACHE[key] = build_nc(DEBUG)
    return _BUILD_CACHE[key]


def kernel(**inputs):
    nc, _ = get_nc()
    in_maps = make_in_maps(inputs)
    res = run_bass_kernel_spmd(nc, in_maps, core_ids=list(range(8)))
    out = np.empty((4, S_FULL, D), dtype=np.float32)
    for c in range(8):
        b, h = c // 2, c % 2
        out[b, h * TCORE:(h + 1) * TCORE] = res.results[c]["out"]
    return out
